# version 07
# baseline (speedup 1.0000x reference)
import math
import contextlib
import numpy as np
import ml_dtypes
import concourse.bass as bass
import concourse.mybir as mybir
from concourse.bass_utils import run_bass_kernel_spmd

F32 = mybir.dt.float32
BF16 = mybir.dt.bfloat16
U8 = mybir.dt.uint8
AF = mybir.ActivationFunctionType
ALU = mybir.AluOpType
AX = mybir.AxisListType

D = 1024
T = 2048
NB = 8
BS = 256
NH = 16
E = 8
DFF = 4096
NFO = DFF // 128
NC = D // 128
TT = 512
NTT = T // TT
EPS = 1e-5
MASKVAL = -1e9

ENGINES = ("sync", "gpsimd", "vector", "scalar", "tensor")
LOOKAHEAD = 2


class Prog:

    def __init__(self):
        self.ops = {e: [] for e in ENGINES}
        self.cnt = {}

    def n(self, sem):
        return self.cnt.get(sem, 0)

    def emit(self, eng, fn, waits=(), inc=None, inck=1):
        ws = [(s, v) for (s, v) in waits if v > 0]
        rec = (fn, ws, (inc, inck) if inc is not None else None)
        if inc is not None:
            self.cnt[inc] = self.cnt.get(inc, 0) + inck
        self.ops[eng].append(rec)

    def dma(self, eng, out, in_, waits=(), inc=None):
        self.emit(eng, lambda e, o=out, i=in_: e.dma_start(out=o, in_=i),
                  waits=waits, inc=inc, inck=16)

    def realize(self, nc):
        names = set(self.cnt.keys())
        for e in ENGINES:
            for (_, ws, _) in self.ops[e]:
                for (s, _) in ws:
                    names.add(s)
        with contextlib.ExitStack() as st:
            hs = {s: st.enter_context(nc.semaphore(s)) for s in sorted(names)}
            with nc.Block() as blk:
                for eng in ENGINES:
                    recs = self.ops[eng]
                    if not recs:
                        continue

                    def make(recs):
                        def body(e):
                            for fn, ws, inc in recs:
                                for (s, v) in ws:
                                    e.wait_ge(hs[s], v)
                                inst = fn(e)
                                if inc is not None:
                                    inst.then_inc(hs[inc[0]], inc[1])
                        return body
                    getattr(blk, eng)(make(recs))



def _rope_tables():
    inv = 10000.0 ** (-np.arange(0, 64, 2, dtype=np.float64) / 64)
    t = np.arange(T, dtype=np.float64)
    ang = t[None, :] * inv[:, None]
    C = np.cos(ang)
    S = np.sin(ang)
    CS = np.tile(C, (4, 1))
    SN2 = np.concatenate([-S, S, -S, S], axis=0)
    sc = 0.125
    b = ml_dtypes.bfloat16
    return (np.ascontiguousarray((CS * sc).astype(b)),
            np.ascontiguousarray((SN2 * sc).astype(b)),
            np.ascontiguousarray(CS.astype(b)),
            np.ascontiguousarray(SN2.astype(b)))


def _perm_qk():
    p = []
    for c in range(NC):
        for h in (2 * c, 2 * c + 1):
            p += [64 * h + 2 * i for i in range(32)]
            p += [64 * h + 2 * i + 1 for i in range(32)]
    return np.array(p)


def prep_inputs(L0_attn_x, L0_attn_Wqkv, L0_attn_Wout, L0_ffn_Wr, L0_ffn_br,
                L0_ffn_W1, L0_ffn_b1, L0_ffn_W2, L0_ffn_b2):
    f = np.float32
    x = np.asarray(L0_attn_x, f)
    Wqkv = np.asarray(L0_attn_Wqkv, f)
    Wout = np.asarray(L0_attn_Wout, f)
    Wr = np.asarray(L0_ffn_Wr, f)
    br = np.asarray(L0_ffn_br, f)
    W1 = np.asarray(L0_ffn_W1, f)
    b1 = np.asarray(L0_ffn_b1, f)
    W2 = np.asarray(L0_ffn_W2, f)
    b2 = np.asarray(L0_ffn_b2, f)

    perm = _perm_qk()
    full_perm = np.concatenate([perm, 1024 + perm, 2048 + np.arange(1024)])
    wqkvt = np.ascontiguousarray(Wqkv[full_perm].T)
    woutt = np.ascontiguousarray(Wout.T)
    w1h = np.ascontiguousarray(
        W1.reshape(E, 8, 128, NFO, 128).transpose(0, 3, 2, 1, 4)
        .reshape(E, NFO, 128, 1024))
    b1h = np.ascontiguousarray(b1.reshape(E, NFO, 128).transpose(0, 2, 1))
    w2h = np.ascontiguousarray(
        W2.reshape(E, NFO, 128, 8, 128).transpose(0, 2, 1, 3, 4)
        .reshape(E, 128, 32768))
    b2h = np.ascontiguousarray(b2.reshape(E, 8, 128).transpose(0, 2, 1))
    brh = np.ascontiguousarray(br.reshape(8, 1))

    csq, snq, csk, snk = _rope_tables()
    bfe = ml_dtypes.bfloat16
    shared = dict(
        wqkvt=wqkvt, woutt=woutt, wr=np.ascontiguousarray(Wr), brh=brh,
        w1=w1h, b1=b1h, w2=w2h, b2=b2h,
        csq=csq, snq=snq, csk=csk, snk=snk,
        ibf=np.eye(128, dtype=bfe),
        if32=np.eye(128, dtype=f),
        maskc=np.concatenate([
            _mk := np.where(np.arange(128)[None, :] <= np.arange(128)[:, None],
                            0.0, MASKVAL).astype(bfe),
            np.zeros((128, 128), dtype=bfe), _mk], axis=1),
        onescol=np.ones((128, 1), dtype=bfe),
        epsv=np.full((1, 1), EPS, dtype=f),
        onesrowf=np.ones((1, 128), dtype=f),
        onesrow=np.ones((1, 128), dtype=bfe))
    maps = []
    for bi in range(x.shape[0]):
        m = dict(shared)
        m["x"] = np.ascontiguousarray(x[bi].T)
        maps.append(m)
    return maps



def build_program(stage="full"):
    nc = bass.Bass("TRN2", target_bir_lowering=False, debug=False)

    d_in = {}
    for name, shape, dt in [
        ("x", [D, T], F32), ("wqkvt", [D, 3 * D], F32), ("woutt", [D, D], F32),
        ("wr", [D, 8], F32), ("brh", [8, 1], F32),
        ("w1", [E, NFO, 128, 1024], F32), ("b1", [E, 128, NFO], F32),
        ("w2", [E, 128, 32768], F32), ("b2", [E, 128, 8], F32),
        ("csq", [128, T], BF16), ("snq", [128, T], BF16),
        ("csk", [128, T], BF16), ("snk", [128, T], BF16),
        ("ibf", [128, 128], BF16), ("if32", [128, 128], F32),
        ("maskc", [128, 384], BF16),
        ("onescol", [128, 1], BF16), ("onesrow", [1, 128], BF16),
        ("epsv", [1, 1], F32), ("onesrowf", [1, 128], F32),
    ]:
        d_in[name] = nc.dram_tensor(name, shape, dt, kind="ExternalInput").ap()
    y_d = nc.dram_tensor("y", [D, T], F32, kind="ExternalOutput").ap()
    h1d = nc.dram_tensor("h1d", [NTT, NFO, 128, TT], BF16, kind="Internal").ap()
    x1d = nc.dram_tensor("x1d", [NC, 128, T], F32, kind="Internal").ap()

    dbg = {}
    if stage != "full":
        dbg["rs"] = nc.dram_tensor("dbg_rs", [128, NC * T], F32,
                                   kind="ExternalOutput").ap()
        dbg["aux"] = nc.dram_tensor("dbg_aux", [128, 8192], F32,
                                    kind="ExternalOutput").ap()

    SBBASE = 24576

    def A(name, shape, dt, offset):
        return nc.alloc_sbuf_tensor_at(name, shape, dt, offset=offset + SBBASE)
    c_ibf = A("c_ibf", [128, 128], BF16, offset=0)
    c_mask = A("c_mask", [128, 384], BF16, offset=256)
    c_if32 = A("c_if32", [128, 128], F32, offset=1024)
    c_ones_col = A("c_ones_col", [128, 1], BF16, offset=1536)
    c_ones_row = A("c_ones_row", [1, 128], BF16, offset=1568)
    c_eps = A("c_eps", [1, 1], F32, offset=1824)
    c_ones_row_f = A("c_ones_row_f", [1, 128], F32, offset=1856)
    wr_sb = A("wr_sb", [128, 64], F32, offset=2368)
    br_sb = A("br_sb", [8, 1], F32, offset=2624)
    b1_sb = A("b1_sb", [128, 64], F32, offset=2656)
    b2_sb = A("b2_sb", [128, 16], F32, offset=2912)
    rs32 = A("rs32", [128, NC * T], F32, offset=4096)
    w2buf = A("w2buf", [128, 32768], BF16, offset=4096)
    wq_sb = A("wq_sb", [128, NC * 3072], BF16, offset=69632)
    wo_sb = A("wo_sb", [128, 8192], BF16, offset=118784)
    x1bf = A("x1bf", [128, NC * T], BF16, offset=69632)
    ybf = A("ybf", [128, NC * T], BF16, offset=102400)
    csq_sb = A("csq_sb", [128, T], BF16, offset=135168)
    snq_sb = A("snq_sb", [128, T], BF16, offset=139264)
    csk_sb = A("csk_sb", [128, T], BF16, offset=143360)
    snk_sb = A("snk_sb", [128, T], BF16, offset=147456)
    masku8 = [A(f"masku8{i}", [128, T], U8, offset=135168 + 2048 * i)
              for i in range(2)]
    gatebc = A("gatebc", [128, T], BF16, offset=139264)
    srow = A("srow", [1, 2048], F32, offset=143360)
    xbf = A("xbf", [128, T], BF16, offset=151552)
    qraw = A("qraw", [128, T], BF16, offset=155648)
    kraw = A("kraw", [128, T], BF16, offset=159744)
    sraw = A("sraw", [128, T], BF16, offset=163840)
    vpadO = A("vpadO", [128, T], BF16, offset=167936)
    kpe = A("kpe", [128, T], BF16, offset=172032)
    kpo = A("kpo", [128, T], BF16, offset=176128)
    vpadE = A("vpadE", [128, T], BF16, offset=180224)
    oblk = A("oblk", [128, T], BF16, offset=184320)
    p_sb = [A(f"p_sb{i}", [128, 384], BF16, offset=188416 + 768 * i)
            for i in range(2)]
    pt_sb = [A(f"pt_sb{i}", [128, 384], BF16, offset=189952 + 768 * i)
             for i in range(2)]
    d_sb = [A(f"d_sb{i}", [128, 256], BF16, offset=191488 + 512 * i)
            for i in range(2)]
    nm_sb = A("nm_sb", [128, 4], F32, offset=192512)
    ss_sb = A("ss_sb", [128, 4], F32, offset=192544)
    rc_sb = A("rc_sb", [128, 4], F32, offset=192576)
    w1slab = [A(f"w1slab{i}", [128, 1024], BF16, offset=151552 + 2048 * i)
              for i in range(2)]
    h1stage = [A(f"h1stage{i}", [128, T], BF16, offset=155648 + 4096 * i)
               for i in range(2)]
    h1in = [A(f"h1in{i}", [128, TT], BF16,
              offset=(163840 + 1024 * i) if i < 4 else
                     (192608 + 1024 * (i - 4)))
            for i in range(8)]
    yestage = [A(f"yestage{i}", [128, TT], BF16, offset=167936 + 1024 * i)
               for i in range(4)]
    logits = A("logits", [8, T], F32, offset=172032)
    lt_tm = A("lt_tm", [128, 8], F32, offset=180224)
    lt_m = A("lt_m", [128, 8], F32, offset=180256)
    rnm = A("rnm", [128, 16], F32, offset=180352)
    rmx = A("rmx", [128, 16], F32, offset=180416)
    res_ = A("res_", [128, 16], F32, offset=180480)
    rgt = A("rgt", [128, 16], F32, offset=180544)
    rexp = A("rexp", [128, 8], F32, offset=180608)
    mrow = A("mrow", [1, T], BF16, offset=180672)
    grow = A("grow", [1, T], BF16, offset=184768)
    lnbf = [A(f"lnbf{i}", [128, TT], BF16, offset=151552 + 1024 * i)
            for i in range(2)]
    sqbf = [A(f"sqbf{i}", [128, TT], BF16, offset=153600 + 1024 * i)
            for i in range(2)]
    mbc = A("mbc", [128, TT], F32, offset=188416)
    rbc = A("rbc", [128, TT], F32, offset=190464)
    t1f = A("t1f", [128, T], F32, offset=172032)

    pb = [nc.alloc_psum_tensor(f"pb{i}", [128, 512], F32) for i in range(8)]

    P = Prog()

    for name, dst in [("ibf", c_ibf), ("maskc", c_mask), ("if32", c_if32),
                      ("onescol", c_ones_col), ("onesrow", c_ones_row),
                      ("brh", br_sb),
                      ("csq", csq_sb), ("snq", snq_sb),
                      ("csk", csk_sb), ("snk", snk_sb), ("epsv", c_eps),
                      ("onesrowf", c_ones_row_f)]:
        P.dma("sync", dst[:], d_in[name][:], inc="ldc")
    for k in range(8):
        P.dma("sync", wr_sb[:, 8 * k:8 * k + 8],
              d_in["wr"][128 * k:128 * (k + 1), :], inc="ldc")
    LDC_ALL = 16 * 20
    for c in range(NC):
        P.dma("sync", rs32[:, T * c:T * (c + 1)],
              d_in["x"][128 * c:128 * (c + 1), :], inc="ldx")
    for fc in range(NC):
        P.dma("gpsimd", wq_sb[:, 3072 * fc:3072 * (fc + 1)],
              d_in["wqkvt"][128 * fc:128 * (fc + 1), :], inc="ldw")
    for dc in range(NC):
        P.dma("gpsimd", wo_sb[:, 1024 * dc:1024 * (dc + 1)],
              d_in["woutt"][128 * dc:128 * (dc + 1), :], inc="ldw")
    LDW_ALL = 16 * 16
    P.emit("gpsimd", lambda e: e.memset(kpe[64:128, :], 0.0), inc="gz")
    P.emit("gpsimd", lambda e: e.memset(kpo[0:64, :], 0.0), inc="gz")
    P.emit("gpsimd", lambda e: e.memset(
        vpadE[:].rearrange("p (g c) -> p g c", g=16)[:, :, 64:128], 0.0),
        inc="gz")
    P.emit("gpsimd", lambda e: e.memset(
        vpadO[:].rearrange("p (g c) -> p g c", g=16)[:, :, 0:64], 0.0),
        inc="gz")
    GZ_PADS = 4

    def rope(src, cs, sn, blk, evac_wait, is_k):
        P.dma("sync", sraw[0:32, :], src[32:64, :],
              waits=[evac_wait, ("rope", P.n("rope"))], inc="swp")
        P.dma("sync", sraw[32:64, :], src[0:32, :], inc="swp")
        P.dma("sync", sraw[64:96, :], src[96:128, :], inc="swp")
        P.dma("sync", sraw[96:128, :], src[64:96, :], inc="swp")

        def tbl(t_):
            return t_[:, BS * blk:BS * (blk + 1)].unsqueeze(1) \
                .broadcast_to([128, 8, BS])

        def r3(ap):
            return ap.rearrange("p (c t) -> p c t", c=8)
        P.emit("vector", lambda e, s=src, t=tbl(cs):
               e.tensor_tensor(r3(s[:]), r3(s[:]), t, ALU.mult),
               waits=[("swp", P.n("swp")), ("ldc", LDC_ALL),
                      ("sqk", 16 * blk), evac_wait], inc="vself")
        P.emit("vector", lambda e, t=tbl(sn):
               e.tensor_tensor(r3(sraw[:]), r3(sraw[:]), t, ALU.mult),
               inc="vself")
        vw = ("vself", P.n("vself"))
        if not is_k:
            P.emit("vector", lambda e, s=src:
                   e.tensor_add(s[:], s[:], sraw[:]),
                   waits=[vw], inc="rope")
        else:
            P.emit("vector", lambda e, s=src:
                   e.tensor_add(kpe[0:64, :], s[0:64, :], sraw[0:64, :]),
                   waits=[vw, ("gz", GZ_PADS)])
            P.emit("vector", lambda e, s=src:
                   e.tensor_add(kpo[64:128, :], s[64:128, :],
                                sraw[64:128, :]), inc="rope")

    for blk in range(NB):
        xc_waits = [("ldx", 16 * NC), ("qkmm", P.n("qkmm"))]
        for c in range(NC):
            P.emit("vector", lambda e, c=c, blk=blk: e.tensor_copy(
                xbf[:, BS * c:BS * (c + 1)],
                rs32[:, T * c + BS * blk:T * c + BS * (blk + 1)]),
                waits=xc_waits if c == 0 else [],
                inc="xc" if c == NC - 1 else None)
        qke0 = P.n("qke")
        for g in range(20):
            b = g % 8
            waits = [("xc", P.n("xc")), ("ldw", LDW_ALL),
                     ("r1add", P.n("r1add"))]
            if P.n("qke") - qke0 >= 8:
                waits.append(("qke", P.n("qke") - 7))
            if g < 16:
                jc = g
                for fc in range(NC):
                    P.emit("tensor", lambda e, b=b, jc=jc, fc=fc:
                           e.matmul(pb[b][:, 0:256],
                                    wq_sb[:, 3072 * fc + 128 * jc:
                                          3072 * fc + 128 * (jc + 1)],
                                    xbf[:, BS * fc:BS * (fc + 1)],
                                    start=(fc == 0), stop=(fc == NC - 1)),
                           waits=waits if fc == 0 else [],
                           inc="qkmm" if fc == NC - 1 else None)
                dst, cc = (qraw, jc) if jc < 8 else (kraw, jc - 8)
                P.emit("scalar", lambda e, b=b, dst=dst, cc=cc:
                       e.copy(dst[:, BS * cc:BS * (cc + 1)], pb[b][:, 0:256]),
                       waits=[("qkmm", P.n("qkmm"))], inc="qke")
            else:
                th, jg = (g - 16) // 2, (g - 16) % 2
                for fc in range(NC):
                    P.emit("tensor", lambda e, b=b, th=th, jg=jg, fc=fc:
                           e.matmul(pb[b][:, 0:512],
                                    xbf[:, BS * fc + 128 * th:
                                        BS * fc + 128 * (th + 1)],
                                    wq_sb[:, 3072 * fc + 2048 + 512 * jg:
                                          3072 * fc + 2048 + 512 * (jg + 1)],
                                    start=(fc == 0), stop=(fc == NC - 1)),
                           waits=waits if fc == 0 else [],
                           inc="qkmm" if fc == NC - 1 else None)
                for hh in range(8):
                    h_abs = 8 * jg + hh
                    vdst = vpadE if h_abs % 2 == 0 else vpadO
                    doff = 1024 * th + (h_abs // 2) * 128 + 64 * (h_abs % 2)
                    P.emit("scalar", lambda e, b=b, vdst=vdst, doff=doff,
                           hh=hh:
                           e.copy(vdst[:, doff:doff + 64],
                                  pb[b][:, 64 * hh:64 * (hh + 1)]),
                           waits=[("qkmm", P.n("qkmm"))] if hh == 0 else [],
                           inc="qke" if hh == 7 else None)
        qke_all = P.n("qke")
        rope(qraw, csq_sb, snq_sb, blk, ("qke", qke0 + 8), is_k=False)
        rope(kraw, csk_sb, snk_sb, blk, ("qke", qke0 + 16), is_k=True)

        B = {s: P.n(s) for s in ("sqk", "smax", "sexp", "rcp", "srcp",
                                 "ptmm", "ptea", "avmm", "oevac",
                                 "rope")}
        ROPE2 = B["rope"]

        def s_mm(h):
            par = h % 2
            c = h // 2
            kp = kpe if h % 2 == 0 else kpo
            sbank = pb[par]
            w0 = [("rope", ROPE2), ("gz", GZ_PADS),
                  ("ldc", LDC_ALL), ("qke", qke0 + 18),
                  ("sexp", B["sexp"] + h - 1)]
            P.emit("tensor", lambda e, sbank=sbank: e.matmul(
                sbank[:, 0:384], c_ibf[:], c_mask[:],
                start=True, stop=False), waits=w0)
            P.emit("tensor", lambda e, sbank=sbank, c=c, kp=kp: e.matmul(
                sbank[:, 0:128], qraw[:, BS * c:BS * c + 128],
                kp[:, BS * c:BS * c + 128], start=False, stop=False))
            P.emit("tensor", lambda e, sbank=sbank, c=c, kp=kp: e.matmul(
                sbank[:, 128:384], qraw[:, BS * c + 128:BS * c + 256],
                kp[:, BS * c:BS * (c + 1)], start=False, stop=True),
                inc="sqk")

        def pt_mm(h):
            par = h % 2
            ptb = pb[2 + par]
            wp = [("srcp", B["srcp"] + h + 1),
                  ("ptea", B["ptea"] + h - 1)]
            if h < 2:
                wp.append(("qke", qke0 + 20))
            P.emit("tensor", lambda e, ptb=ptb, par=par: e.matmul(
                ptb[:, 0:128], p_sb[par][:, 0:128], d_sb[par][:, 0:128],
                start=True, stop=True), waits=wp)
            P.emit("tensor", lambda e, ptb=ptb, par=par: e.matmul(
                ptb[:, 128:256], p_sb[par][:, 128:256],
                d_sb[par][:, 128:256], start=True, stop=True))
            P.emit("tensor", lambda e, ptb=ptb, par=par: e.matmul(
                ptb[:, 256:384], p_sb[par][:, 256:384],
                d_sb[par][:, 128:256], start=True, stop=True), inc="ptmm")

        def av_mm(c):
            ob = pb[4 + c % 2]
            wa = [("ptea", B["ptea"] + 2 * c + 2),
                  ("oevac", B["oevac"] + c - 1)]
            if c == 0:
                wa.append(("qke", qke_all))
            P.emit("tensor", lambda e, ob=ob, c=c: e.matmul(
                ob[:, 0:256], vpadE[:, 128 * c:128 * (c + 1)],
                pt_sb[0][:, 0:256], start=True, stop=False), waits=wa)
            P.emit("tensor", lambda e, ob=ob, c=c: e.matmul(
                ob[:, 0:256], vpadO[:, 128 * c:128 * (c + 1)],
                pt_sb[1][:, 0:256], start=False, stop=False))
            P.emit("tensor", lambda e, ob=ob, c=c: e.matmul(
                ob[:, 128:256], vpadE[:, 1024 + 128 * c:1024 + 128 * (c + 1)],
                pt_sb[0][:, 256:384], start=False, stop=False))
            P.emit("tensor", lambda e, ob=ob, c=c: e.matmul(
                ob[:, 128:256], vpadO[:, 1024 + 128 * c:1024 + 128 * (c + 1)],
                pt_sb[1][:, 256:384], start=False, stop=True), inc="avmm")

        def dve_max(h):
            par = h % 2
            sbank = pb[par]
            P.emit("vector", lambda e, sbank=sbank, par=par: e.reduce_max(
                nm_sb[:, 2 * par:2 * par + 1], sbank[:, 0:128], axis=AX.X,
                negate=True),
                waits=[("sqk", B["sqk"] + h + 1),
                       ("sexp", B["sexp"] + h - 1)])
            P.emit("vector", lambda e, sbank=sbank, par=par: e.reduce_max(
                nm_sb[:, 2 * par + 1:2 * par + 2], sbank[:, 128:384],
                axis=AX.X, negate=True), inc="smax")

        def dve_recip(h):
            par = h % 2
            P.emit("vector", lambda e, par=par: e.reciprocal(
                rc_sb[:, 2 * par:2 * par + 2], ss_sb[:, 2 * par:2 * par + 2]),
                waits=[("sexp", B["sexp"] + h + 1)], inc="rcp")

        def pool_d(h, eng="gpsimd"):
            par = h % 2
            P.emit(eng, lambda e, par=par: e.tensor_scalar(
                d_sb[par][:, 0:128], c_ibf[:], rc_sb[:, 2 * par:2 * par + 1],
                None, ALU.mult),
                waits=[("rcp", B["rcp"] + h + 1),
                       ("ptmm", B["ptmm"] + h - 1)])
            P.emit(eng, lambda e, par=par: e.tensor_scalar(
                d_sb[par][:, 128:256], c_ibf[:],
                rc_sb[:, 2 * par + 1:2 * par + 2], None, ALU.mult),
                inc="srcp")

        def act_exp(h):
            par = h % 2
            sbank = pb[par]
            we = [("smax", B["smax"] + h + 1),
                  ("ptmm", B["ptmm"] + h - 1),
                  ("rcp", B["rcp"] + h - 1)]
            P.emit("scalar", lambda e, sbank=sbank, par=par: e.activation(
                p_sb[par][:, 0:128], sbank[:, 0:128], AF.Exp,
                bias=nm_sb[:, 2 * par:2 * par + 1],
                accum_out=ss_sb[:, 2 * par:2 * par + 1]), waits=we)
            P.emit("scalar", lambda e, sbank=sbank, par=par: e.activation(
                p_sb[par][:, 128:384], sbank[:, 128:384], AF.Exp,
                bias=nm_sb[:, 2 * par + 1:2 * par + 2],
                accum_out=ss_sb[:, 2 * par + 1:2 * par + 2]), inc="sexp")

        def pte(h):
            par = h % 2
            ptb = pb[2 + par]
            P.emit("scalar", lambda e, ptb=ptb, par=par: e.copy(
                pt_sb[par][:, 0:384], ptb[:, 0:384]),
                waits=[("ptmm", B["ptmm"] + h + 1),
                       ("avmm", B["avmm"] + h // 2)], inc="ptea")

        def o_evac(c):
            ob = pb[4 + c % 2]
            P.emit("scalar", lambda e, ob=ob, c=c: e.copy(
                oblk[:, BS * c:BS * (c + 1)], ob[:, 0:256]),
                waits=[("avmm", B["avmm"] + c + 1)], inc="oevac")

        if LOOKAHEAD:
            s_mm(0)
            s_mm(1)
            for h in range(NH):
                if h + 2 < NH:
                    s_mm(h + 2)
                pt_mm(h)
                if h % 2 == 1:
                    av_mm(h // 2)
        else:
            for h in range(NH):
                s_mm(h)
                pt_mm(h)
                if h % 2 == 1:
                    av_mm(h // 2)
        dve_max(0)
        dve_max(1)
        for h in range(NH):
            dve_recip(h)
            pool_d(h, eng="vector")
            if h + 2 < NH:
                dve_max(h + 2)
        for h in range(NH):
            act_exp(h)
            if h >= 1:
                pte(h - 1)
                if (h - 1) % 2 == 1:
                    o_evac((h - 1) // 2)
        pte(NH - 1)
        o_evac(NH // 2 - 1)

        for jc in range(NC):
            b = jc % 8
            ww = [("oevac", P.n("oevac")), ("ldw", LDW_ALL)]
            for dc in range(NC):
                P.emit("tensor", lambda e, b=b, jc=jc, dc=dc: e.matmul(
                    pb[b][:, 0:256],
                    wo_sb[:, 1024 * dc + 128 * jc:1024 * dc + 128 * (jc + 1)],
                    oblk[:, BS * dc:BS * (dc + 1)],
                    start=(dc == 0), stop=(dc == NC - 1)),
                    waits=ww if dc == 0 else [],
                    inc="womm" if dc == NC - 1 else None)
            P.emit("vector", lambda e, b=b, jc=jc, blk=blk: e.tensor_add(
                rs32[:, T * jc + BS * blk:T * jc + BS * (blk + 1)],
                rs32[:, T * jc + BS * blk:T * jc + BS * (blk + 1)],
                pb[b][:, 0:256]),
                waits=[("womm", P.n("womm"))], inc="r1add")

    if stage == "attn":
        P.dma("sync", dbg["rs"][:], rs32[:],
              waits=[("r1add", P.n("r1add"))], inc="sto")
        P.emit("gpsimd", lambda e: e.engine_nop(),
               waits=[("sto", P.n("sto"))])
        return nc, P, dbg

    def layer_norm(gate, emit_x1bf, emit_spill):
        for tg in range(NTT):
            gcols = slice(TT * tg, TT * (tg + 1))
            for c in range(NC):
                sl = c % 2
                wv = [gate(tg) if callable(gate) else gate]
                if P.n("lnpe") >= 2:
                    wv.append(("lnpe", P.n("lnpe") - 1))
                if P.n("lnsq") >= 2:
                    wv.append(("lnsq", P.n("lnsq") - 1))
                P.emit("vector", lambda e, c=c, sl=sl, g=gcols: e.tensor_copy(
                    lnbf[sl][:], rs32[:, T * c:T * (c + 1)][:, g]),
                    waits=wv, inc="lncast")
                sqw = [("lncast", P.n("lncast"))]
                if P.n("lnpe") >= 2:
                    sqw.append(("lnpe", P.n("lnpe") - 1))
                P.emit("scalar", lambda e, sl=sl: e.activation(
                    sqbf[sl][:], lnbf[sl][:], AF.Square),
                    waits=sqw, inc="lnsq")
                mmw = [("lncast", P.n("lncast")), ("ldc", LDC_ALL)]
                if c == 0:
                    mmw += [("lnbce", P.n("lnbce")), ("lnsev", P.n("lnsev"))]
                P.emit("tensor", lambda e, sl=sl, c=c: e.matmul(
                    pb[6][0:1, 0:TT], c_ones_col[:], lnbf[sl][:],
                    start=(c == 0), stop=(c == NC - 1)), waits=mmw)
                P.emit("tensor", lambda e, sl=sl, c=c: e.matmul(
                    pb[7][0:1, 0:TT], c_ones_col[:], sqbf[sl][:],
                    start=(c == 0), stop=(c == NC - 1)),
                    waits=[("lnsq", P.n("lnsq"))], inc="lnpe")
            sA = srow[0:1, 0:512]
            sB = srow[0:1, 512:1024]
            sM = srow[0:1, 1024:1536]
            sV = srow[0:1, 1536:2048]
            P.emit("vector", lambda e: e.tensor_copy(sA, pb[6][0:1, 0:TT]),
                   waits=[("lnpe", P.n("lnpe"))], inc="vself")
            P.emit("vector", lambda e: e.tensor_copy(sB, pb[7][0:1, 0:TT]),
                   inc="lnsev")
            P.emit("vector", lambda e: e.tensor_scalar_mul(sM, sA, 1.0 / D),
                   waits=[("vself", P.n("vself")), ("lnsev", P.n("lnsev")),
                          ("lnbcm", P.n("lnbcm"))], inc="vself")
            P.emit("scalar", lambda e: e.activation(sV, sM, AF.Square),
                   waits=[("vself", P.n("vself")),
                          ("lnbcm", P.n("lnbcm"))], inc="aself")
            P.emit("vector", lambda e: e.scalar_tensor_tensor(
                sV, sB, 1.0 / D, sV, ALU.mult, ALU.subtract),
                waits=[("aself", P.n("aself")), ("lnsev", P.n("lnsev"))],
                inc="lnst")
            P.emit("scalar", lambda e: e.activation(sV, sV, AF.Ln, bias=c_eps[0:1, 0:1]),
                   waits=[("lnst", P.n("lnst"))], inc="aself")
            P.emit("scalar", lambda e: e.activation(sV, sV, AF.Exp,
                                                    scale=-0.5),
                   waits=[("aself", P.n("aself"))], inc="lnrstd")
            P.emit("tensor", lambda e: e.matmul(
                pb[6][:, 0:TT], c_ones_row_f[0:1, :], sM, start=True,
                stop=True), waits=[("lnrstd", P.n("lnrstd"))])
            P.emit("tensor", lambda e: e.matmul(
                pb[7][:, 0:TT], c_ones_row_f[0:1, :], sV, start=True,
                stop=True), inc="lnbcm")
            P.emit("vector", lambda e: e.tensor_copy(mbc[:], pb[6][:, 0:TT]),
                   waits=[("lnbcm", P.n("lnbcm"))], inc="vself")
            P.emit("vector", lambda e: e.tensor_copy(rbc[:], pb[7][:, 0:TT]),
                   inc="lnbce")
            for c in range(NC):
                view = rs32[:, T * c:T * (c + 1)][:, gcols]
                P.emit("vector", lambda e, v=view: e.tensor_sub(v, v, mbc[:]),
                       waits=[("lnbce", P.n("lnbce")),
                              ("vself", P.n("vself"))], inc="vself")
                P.emit("vector", lambda e, v=view: e.tensor_mul(v, v, rbc[:]),
                       waits=[("vself", P.n("vself"))], inc="lnnorm")
                if emit_x1bf:
                    P.emit("vector", lambda e, v=view, c=c, g=gcols:
                           e.tensor_copy(x1bf[:, T * c:T * (c + 1)][:, g], v),
                           waits=[("lnnorm", P.n("lnnorm"))], inc="x1c")
        if emit_spill:
            for c in range(NC):
                P.dma("sync", x1d[c, :, :], rs32[:, T * c:T * (c + 1)],
                      waits=[("lnnorm", P.n("lnnorm"))], inc="spx1")

    layer_norm(("r1add", P.n("r1add")), emit_x1bf=True, emit_spill=True)
    LNNORM1 = P.n("lnnorm")
    X1C_ALL = P.n("x1c")

    for tg in range(NTT):
        gcols = slice(TT * tg, TT * (tg + 1))
        rw = [("lnnorm", LNNORM1), ("ldc", LDC_ALL), ("rtev", P.n("rtev"))]
        for k in range(8):
            P.emit("tensor", lambda e, k=k, g=gcols: e.matmul(
                pb[4][0:8, 0:TT], wr_sb[:, 8 * k:8 * (k + 1)],
                rs32[:, T * k:T * (k + 1)][:, g],
                start=(k == 0), stop=(k == 7)),
                waits=rw if k == 0 else [],
                inc="rtmm" if k == 7 else None)
        P.emit("scalar", lambda e, g=gcols: e.activation(
            logits[0:8, g], pb[4][0:8, 0:TT], AF.Identity, bias=br_sb[0:8, 0:1]),
            waits=[("rtmm", P.n("rtmm"))], inc="rtev")
    RTEV_ALL = P.n("rtev")
    for cc in range(16):
        P.emit("tensor", lambda e, cc=cc: e.transpose(
            pb[5][:, 0:8], logits[0:8, 128 * cc:128 * (cc + 1)],
            c_if32[0:8, 0:8]),
            waits=[("rtev", RTEV_ALL), ("rtc", P.n("rtc"))], inc="rtc")
        P.emit("vector", lambda e: e.tensor_copy(lt_tm[:], pb[5][:, 0:8]),
               waits=[("rtc", P.n("rtc"))], inc="rtc")
        P.emit("vector", lambda e, cc=cc: e.reduce_max(
            rnm[:, cc:cc + 1], lt_tm[:], axis=AX.X, negate=True),
            waits=[("rtc", P.n("rtc"))], inc="rtc")
        P.emit("scalar", lambda e, cc=cc: e.activation(
            rexp[:], lt_tm[:], AF.Exp, bias=rnm[:, cc:cc + 1],
            accum_out=res_[:, cc:cc + 1]),
            waits=[("rtc", P.n("rtc"))], inc="rtc")
        P.emit("vector", lambda e, cc=cc: e.reciprocal(
            rgt[:, cc:cc + 1], res_[:, cc:cc + 1]),
            waits=[("rtc", P.n("rtc"))], inc="rtc")
        P.emit("vector", lambda e, cc=cc: e.tensor_scalar_mul(
            rmx[:, cc:cc + 1], rnm[:, cc:cc + 1], -1.0),
            waits=[("rtc", P.n("rtc"))], inc="rtc")
        P.emit("vector", lambda e, cc=cc: e.tensor_scalar(
            lt_m[:], lt_tm[:], rmx[:, cc:cc + 1], None, ALU.is_equal),
            waits=[("rtc", P.n("rtc"))], inc="rtc")
        P.emit("tensor", lambda e, cc=cc: e.transpose(
            pb[5][0:8, 8:136], lt_m[:], c_if32[:]),
            waits=[("rtc", P.n("rtc"))], inc="rtc")
        P.emit("vector", lambda e, cc=cc: e.tensor_copy(
            logits[0:8, 128 * cc:128 * (cc + 1)], pb[5][0:8, 8:136]),
            waits=[("rtc", P.n("rtc"))], inc="rtc")
        P.emit("tensor", lambda e, cc=cc: e.transpose(
            pb[5][0:1, 136:264], rgt[:, cc:cc + 1], c_if32[:]),
            waits=[("rtc", P.n("rtc"))], inc="rtc")
        P.emit("vector", lambda e, cc=cc: e.tensor_copy(
            grow[0:1, 128 * cc:128 * (cc + 1)], pb[5][0:1, 136:264]),
            waits=[("rtc", P.n("rtc"))], inc="rtc")
    RTC_ALL = P.n("rtc")
    for tg in range(NTT):
        P.emit("tensor", lambda e, tg=tg: e.matmul(
            pb[5][:, 0:TT], c_ones_row[0:1, :],
            grow[0:1, TT * tg:TT * (tg + 1)], start=True, stop=True),
            waits=[("rtc", RTC_ALL), ("gbce", P.n("gbce"))], inc="gbcm")
        P.emit("vector", lambda e, tg=tg: e.tensor_copy(
            gatebc[:, TT * tg:TT * (tg + 1)], pb[5][:, 0:TT]),
            waits=[("gbcm", P.n("gbcm"))], inc="gbce")
    GBCE_ALL = P.n("gbce")

    if stage == "ln1":
        P.dma("sync", dbg["rs"][:], rs32[:],
              waits=[("lnnorm", LNNORM1)], inc="sto")
        P.dma("gpsimd", dbg["aux"][:, 0:T], gatebc[:],
              waits=[("gbce", GBCE_ALL)], inc="sto")
        P.dma("sync", dbg["aux"][0:8, T:2 * T], logits[0:8, :],
              waits=[("rtc", RTC_ALL)], inc="sto")
        P.dma("sync", dbg["aux"][:, 2 * T:2 * T + TT], mbc[:],
              waits=[("lnnorm", LNNORM1)], inc="sto")
        P.dma("sync", dbg["aux"][:, 2 * T + TT:2 * T + 2 * TT], rbc[:],
              waits=[("lnnorm", LNNORM1)], inc="sto")
        P.dma("sync", dbg["aux"][0:1, 2 * T + 2 * TT:2 * T + 2 * TT + 2048],
              srow[0:1, :], waits=[("lnnorm", LNNORM1)], inc="sto")
        P.emit("gpsimd", lambda e: e.engine_nop(),
               waits=[("sto", P.n("sto"))])
        return nc, P, dbg

    P.emit("gpsimd", lambda e: e.memset(ybf[:], 0.0),
           waits=[("r1add", P.n("r1add"))], inc="ybz")

    for e_ in range(E):
        par = e_ % 2
        P.dma("sync", b1_sb[:, 32 * par:32 * (par + 1)],
              d_in["b1"][e_, :, :],
              waits=[("h1ev", max(0, P.n("h1ev") - 128))], inc=f"b1ld{par}")
        P.dma("sync", b2_sb[:, 8 * par:8 * (par + 1)],
              d_in["b2"][e_, :, :],
              waits=[("yeev", max(0, P.n("yeev") - 32))], inc=f"b2ld{par}")
        P.dma("gpsimd", mrow[0:1, :], logits[e_:e_ + 1, :],
              waits=[("rtc", RTC_ALL), ("mbcm", P.n("mbcm"))], inc="mrld")
        for tg in range(NTT):
            bnk = 6 + tg % 2
            P.emit("tensor", lambda e, tg=tg, bnk=bnk: e.matmul(
                pb[bnk][:, 0:TT], c_ones_row[0:1, :],
                mrow[0:1, TT * tg:TT * (tg + 1)], start=True, stop=True),
                waits=[("mrld", P.n("mrld")), ("lnbce", P.n("lnbce")),
                       ("mbce", max(0, P.n("mbce") - 1)),
                       ("h1ev", P.n("h1ev")), ("yeev", P.n("yeev"))],
                inc="mbcm")
            P.emit("vector", lambda e, tg=tg, par=par, bnk=bnk: e.tensor_copy(
                masku8[par][:, TT * tg:TT * (tg + 1)], pb[bnk][:, 0:TT]),
                waits=[("mbcm", P.n("mbcm")),
                       ("comb", max(0, P.n("comb") - 32))], inc="mbce")
        spill_counts = {}
        hld_wait = {}
        for fo in range(NFO):
            sl = fo % 2
            P.dma("gpsimd", w1slab[sl][:], d_in["w1"][e_, fo, :, :],
                  waits=[("ph1mm", max(0, P.n("ph1mm") - 1)),
                         ("lncast", P.n("lncast")),
                         ("lnpe", P.n("lnpe")), ("lnsq", P.n("lnsq"))],
                  inc=f"w1ld{sl}")
            if fo < 16:
                if e_ == 0:
                    w2w = ([("rtmm", P.n("rtmm")), ("spx1", P.n("spx1")),
                            ("lnnorm", LNNORM1), ("x1c", X1C_ALL)]
                           if fo == 0 else [])
                else:
                    w2w = [("ph2", 128 * e_)] if fo == 0 else []
                P.dma("gpsimd", w2buf[:, 2048 * fo:2048 * (fo + 1)],
                      d_in["w2"][e_, :, 2048 * fo:2048 * (fo + 1)],
                      waits=w2w, inc="w2ld")
            mw = [(f"w1ld{sl}", P.n(f"w1ld{sl}")),
                  ("h1ev", max(0, P.n("h1ev") - 4))]
            if fo < 2:
                mw += [("yeev", P.n("yeev")), ("mbce", P.n("mbce"))]
            if fo < 2 and e_ == 0:
                mw += [("lnbce", P.n("lnbce")), ("rtc", RTC_ALL),
                       ("gbce", GBCE_ALL)]
            for k in range(8):
                for g in range(4):
                    gw = list(mw) if (k == 0 and g == 0) else []
                    if fo == 0 and k == 0 and e_ == 0:
                        gw.append(("x1c", min(8 * (g + 1), X1C_ALL)))
                    P.emit("tensor", lambda e, sl=sl, k=k, g=g, fo=fo:
                           e.matmul(pb[4 * (fo % 2) + g][:, 0:TT],
                                    w1slab[sl][:, 128 * k:128 * (k + 1)],
                                    x1bf[:, T * k + TT * g:T * k + TT * (g + 1)],
                                    start=(k == 0), stop=(k == 7)),
                           waits=gw,
                           inc="ph1mm" if (k == 7 and g == 3) else None)
            for g in range(4):
                P.emit("scalar", lambda e, sl=sl, g=g, fo=fo, par=par:
                       e.activation(h1stage[sl][:, TT * g:TT * (g + 1)],
                                    pb[4 * (fo % 2) + g][:, 0:TT], AF.Relu,
                                    bias=b1_sb[:, 32 * par + fo:
                                               32 * par + fo + 1]),
                       waits=[("ph1mm", P.n("ph1mm")),
                              (f"b1ld{par}", P.n(f"b1ld{par}")),
                              (f"h1sp{sl}", P.n(f"h1sp{sl}"))]
                       if g == 0 else [],
                       inc="h1ev")
            P.emit("sync", lambda e, sl=sl, fo=fo: e.dma_start(
                out=h1d[:, fo, :, :].rearrange("g p t -> p g t"),
                in_=h1stage[sl][:].rearrange("p (g t) -> p g t", g=4)),
                waits=[("h1ev", P.n("h1ev"))], inc=f"h1sp{sl}", inck=16)
            spill_counts[fo] = P.n(f"h1sp{sl}")
            if fo < 8:
                P.dma("gpsimd", h1in[fo % 8][:], h1d[0, fo, :, :],
                      waits=[(f"h1sp{fo % 2}", spill_counts[fo]),
                             ("ph2", max(0, P.n("ph2") - 7 + fo))],
                      inc=f"h1lp{fo % 8}")
                hld_wait[fo] = (f"h1lp{fo % 8}", P.n(f"h1lp{fo % 8}"))
        for tt in range(NTT):
            for fo in range(NFO):
                idx = tt * NFO + fo
                buf = idx % 8
                if idx >= 8:
                    P.dma("sync", h1in[buf][:], h1d[tt, fo, :, :],
                          waits=[(f"h1sp{fo % 2}", spill_counts[fo]),
                                 ("ph2", max(0, P.n("ph2") - 7))],
                          inc=f"h1ld{buf}")
                    hld_wait[idx] = (f"h1ld{buf}", P.n(f"h1ld{buf}"))
                for h in range(8):
                    mw2 = [hld_wait[idx]]
                    if fo == 0:
                        mw2.append(("w2ld", P.n("w2ld")))
                        if tt == 0:
                            mw2 += [("h1ev", P.n("h1ev")),
                                    ("mbce", P.n("mbce"))]
                        else:
                            mw2.append(("yeev", P.n("yeev") - 7 + h))
                    P.emit("tensor", lambda e, h=h, fo=fo, buf=buf: e.matmul(
                        pb[h][:, 0:TT],
                        w2buf[:, 128 * (8 * fo + h):128 * (8 * fo + h + 1)],
                        h1in[buf][:], start=(fo == 0), stop=(fo == NFO - 1)),
                        waits=mw2,
                        inc="ph2" if h == 7 else None)
            for h in range(8):
                sbuf_ = yestage[h % 4]
                P.emit("scalar", lambda e, h=h, par=par, sbuf_=sbuf_:
                       e.activation(sbuf_[:], pb[h][:, 0:TT], AF.Identity,
                                    bias=b2_sb[:, 8 * par + h:
                                               8 * par + h + 1]),
                       waits=[("ph2", P.n("ph2")),
                              (f"b2ld{par}", P.n(f"b2ld{par}")),
                              ("comb", max(0, P.n("comb") - 3))],
                       inc="yeev")
                P.emit("vector", lambda e, h=h, tt=tt, par=par, sbuf_=sbuf_:
                       e.copy_predicated(
                           ybf[:, T * h + TT * tt:T * h + TT * (tt + 1)],
                           masku8[par][:, TT * tt:TT * (tt + 1)], sbuf_[:]),
                       waits=[("yeev", P.n("yeev")), ("ybz", 1),
                              ("mbce", P.n("mbce"))], inc="comb")

    COMB_ALL = P.n("comb")
    for c in range(NC):
        P.dma("sync", rs32[:, T * c:T * (c + 1)], x1d[c, :, :],
              waits=[("ph2", P.n("ph2")), ("comb", COMB_ALL)]
              if c == 0 else [], inc="x1ld")
    X1LD_ALL = P.n("x1ld")
    for c in range(NC):
        P.emit("vector", lambda e, c=c: e.tensor_tensor(
            ybf[:, T * c:T * (c + 1)], ybf[:, T * c:T * (c + 1)],
            gatebc[:], ALU.mult),
            waits=[("comb", COMB_ALL), ("gbce", GBCE_ALL)]
            if c == 0 else [], inc="tailmul")
    TMUL = P.n("tailmul")
    for tg in range(NTT):
        for c in range(NC):
            sl_ = slice(T * c + TT * tg, T * c + TT * (tg + 1))
            P.emit("vector", lambda e, s=sl_: e.tensor_add(
                rs32[:, s], rs32[:, s], ybf[:, s]),
                waits=[("x1ld", X1LD_ALL), ("tailmul", TMUL)]
                if (tg == 0 and c == 0) else [], inc="tailadd")
    TA0 = P.n("tailadd") - 4 * NC
    layer_norm(lambda tg: ("tailadd", TA0 + NC * (tg + 1)),
               emit_x1bf=False, emit_spill=False)
    for c in range(NC):
        P.dma("sync", y_d[128 * c:128 * (c + 1), :],
              rs32[:, T * c:T * (c + 1)],
              waits=[("lnnorm", P.n("lnnorm"))] if c == 0 else [], inc="sto")
    P.emit("gpsimd", lambda e: e.engine_nop(),
           waits=[("sto", P.n("sto"))])
    return nc, P, dbg


_CACHED = {}


def _get_nc(stage="full"):
    if stage not in _CACHED:
        nc, P, dbg = build_program(stage)
        P.realize(nc)
        _CACHED[stage] = nc
    return _CACHED[stage]


def kernel(**inputs):
    maps = prep_inputs(**inputs)
    nc = _get_nc("full")
    res = run_bass_kernel_spmd(nc, maps, core_ids=list(range(8)))
    outs = []
    for r in res.results:
        y = np.asarray(r["y"])
        outs.append(y.T)
    return np.ascontiguousarray(np.stack(outs, axis=0))


# version 08
# speedup vs baseline: 1.1775x; 1.1775x over previous
import math
import contextlib
import numpy as np
import ml_dtypes
import concourse.bass as bass
import concourse.mybir as mybir
from concourse.bass_utils import run_bass_kernel_spmd

F32 = mybir.dt.float32
BF16 = mybir.dt.bfloat16
U8 = mybir.dt.uint8
AF = mybir.ActivationFunctionType
ALU = mybir.AluOpType
AX = mybir.AxisListType

D = 1024
T = 2048
NB = 8
BS = 256
NH = 16
E = 8
DFF = 4096
NFO = DFF // 128
NC = D // 128
TT = 512
NTT = T // TT
EPS = 1e-5
MASKVAL = -1e9

ENGINES = ("sync", "gpsimd", "vector", "scalar", "tensor")
LOOKAHEAD = 2


class Prog:

    def __init__(self):
        self.ops = {e: [] for e in ENGINES}
        self.cnt = {}

    def n(self, sem):
        return self.cnt.get(sem, 0)

    def emit(self, eng, fn, waits=(), inc=None, inck=1):
        ws = [(s, v) for (s, v) in waits if v > 0]
        rec = (fn, ws, (inc, inck) if inc is not None else None)
        if inc is not None:
            self.cnt[inc] = self.cnt.get(inc, 0) + inck
        self.ops[eng].append(rec)

    def dma(self, eng, out, in_, waits=(), inc=None):
        self.emit(eng, lambda e, o=out, i=in_: e.dma_start(out=o, in_=i),
                  waits=waits, inc=inc, inck=16)

    def realize(self, nc):
        names = set(self.cnt.keys())
        for e in ENGINES:
            for (_, ws, _) in self.ops[e]:
                for (s, _) in ws:
                    names.add(s)
        with contextlib.ExitStack() as st:
            hs = {s: st.enter_context(nc.semaphore(s)) for s in sorted(names)}
            with nc.Block() as blk:
                for eng in ENGINES:
                    recs = self.ops[eng]
                    if not recs:
                        continue

                    def make(recs):
                        def body(e):
                            for fn, ws, inc in recs:
                                for (s, v) in ws:
                                    e.wait_ge(hs[s], v)
                                inst = fn(e)
                                if inc is not None:
                                    inst.then_inc(hs[inc[0]], inc[1])
                        return body
                    getattr(blk, eng)(make(recs))



def _rope_tables():
    inv = 10000.0 ** (-np.arange(0, 64, 2, dtype=np.float64) / 64)
    t = np.arange(T, dtype=np.float64)
    ang = t[None, :] * inv[:, None]
    C = np.cos(ang)
    S = np.sin(ang)
    CS = np.tile(C, (4, 1))
    SN2 = np.concatenate([-S, S, -S, S], axis=0)
    sc = 0.125
    b = ml_dtypes.bfloat16
    return (np.ascontiguousarray((CS * sc).astype(b)),
            np.ascontiguousarray((SN2 * sc).astype(b)),
            np.ascontiguousarray(CS.astype(b)),
            np.ascontiguousarray(SN2.astype(b)))


def _perm_qk():
    p = []
    for c in range(NC):
        for h in (2 * c, 2 * c + 1):
            p += [64 * h + 2 * i for i in range(32)]
            p += [64 * h + 2 * i + 1 for i in range(32)]
    return np.array(p)


def prep_inputs(L0_attn_x, L0_attn_Wqkv, L0_attn_Wout, L0_ffn_Wr, L0_ffn_br,
                L0_ffn_W1, L0_ffn_b1, L0_ffn_W2, L0_ffn_b2):
    f = np.float32
    x = np.asarray(L0_attn_x, f)
    Wqkv = np.asarray(L0_attn_Wqkv, f)
    Wout = np.asarray(L0_attn_Wout, f)
    Wr = np.asarray(L0_ffn_Wr, f)
    br = np.asarray(L0_ffn_br, f)
    W1 = np.asarray(L0_ffn_W1, f)
    b1 = np.asarray(L0_ffn_b1, f)
    W2 = np.asarray(L0_ffn_W2, f)
    b2 = np.asarray(L0_ffn_b2, f)

    perm = _perm_qk()
    full_perm = np.concatenate([perm, 1024 + perm, 2048 + np.arange(1024)])
    wqkvt = np.ascontiguousarray(Wqkv[full_perm].T)
    woutt = np.ascontiguousarray(Wout.T)
    w1h = np.ascontiguousarray(
        W1.reshape(E, 8, 128, NFO, 128).transpose(0, 3, 2, 1, 4)
        .reshape(E, NFO, 128, 1024))
    b1h = np.ascontiguousarray(b1.reshape(E, NFO, 128).transpose(0, 2, 1))
    w2h = np.ascontiguousarray(
        W2.reshape(E, NFO, 128, 8, 128).transpose(0, 2, 1, 3, 4)
        .reshape(E, 128, 32768))
    b2h = np.ascontiguousarray(b2.reshape(E, 8, 128).transpose(0, 2, 1))
    brh = np.ascontiguousarray(br.reshape(8, 1))

    csq, snq, csk, snk = _rope_tables()
    bfe = ml_dtypes.bfloat16
    shared = dict(
        wqkvt=wqkvt, woutt=woutt, wr=np.ascontiguousarray(Wr), brh=brh,
        w1=w1h, b1=b1h, w2=w2h, b2=b2h,
        csq=csq, snq=snq, csk=csk, snk=snk,
        ibf=np.eye(128, dtype=bfe),
        if32=np.eye(128, dtype=f),
        maskc=np.concatenate([
            _mk := np.where(np.arange(128)[None, :] <= np.arange(128)[:, None],
                            0.0, MASKVAL).astype(bfe),
            np.zeros((128, 128), dtype=bfe), _mk], axis=1),
        onescol=np.ones((128, 1), dtype=bfe),
        epsv=np.full((1, 1), EPS, dtype=f),
        onesrowf=np.ones((1, 128), dtype=f),
        onesrow=np.ones((1, 128), dtype=bfe))
    maps = []
    for bi in range(x.shape[0]):
        m = dict(shared)
        m["x"] = np.ascontiguousarray(x[bi].T)
        maps.append(m)
    return maps



def build_program(stage="full"):
    nc = bass.Bass("TRN2", target_bir_lowering=False, debug=False)

    d_in = {}
    for name, shape, dt in [
        ("x", [D, T], F32), ("wqkvt", [D, 3 * D], F32), ("woutt", [D, D], F32),
        ("wr", [D, 8], F32), ("brh", [8, 1], F32),
        ("w1", [E, NFO, 128, 1024], F32), ("b1", [E, 128, NFO], F32),
        ("w2", [E, 128, 32768], F32), ("b2", [E, 128, 8], F32),
        ("csq", [128, T], BF16), ("snq", [128, T], BF16),
        ("csk", [128, T], BF16), ("snk", [128, T], BF16),
        ("ibf", [128, 128], BF16), ("if32", [128, 128], F32),
        ("maskc", [128, 384], BF16),
        ("onescol", [128, 1], BF16), ("onesrow", [1, 128], BF16),
        ("epsv", [1, 1], F32), ("onesrowf", [1, 128], F32),
    ]:
        d_in[name] = nc.dram_tensor(name, shape, dt, kind="ExternalInput").ap()
    y_d = nc.dram_tensor("y", [D, T], F32, kind="ExternalOutput").ap()
    h1d = nc.dram_tensor("h1d", [NTT, NFO, 128, TT], BF16, kind="Internal").ap()
    x1d = nc.dram_tensor("x1d", [NC, 128, T], F32, kind="Internal").ap()

    dbg = {}
    if stage != "full":
        dbg["rs"] = nc.dram_tensor("dbg_rs", [128, NC * T], F32,
                                   kind="ExternalOutput").ap()
        dbg["aux"] = nc.dram_tensor("dbg_aux", [128, 8192], F32,
                                    kind="ExternalOutput").ap()

    SBBASE = 24576

    def A(name, shape, dt, offset):
        return nc.alloc_sbuf_tensor_at(name, shape, dt, offset=offset + SBBASE)
    c_ibf = A("c_ibf", [128, 128], BF16, offset=0)
    c_mask = A("c_mask", [128, 384], BF16, offset=256)
    c_if32 = A("c_if32", [128, 128], F32, offset=1024)
    c_ones_col = A("c_ones_col", [128, 1], BF16, offset=1536)
    c_ones_row = A("c_ones_row", [1, 128], BF16, offset=1568)
    c_eps = A("c_eps", [1, 1], F32, offset=1824)
    c_ones_row_f = A("c_ones_row_f", [1, 128], F32, offset=1856)
    wr_sb = A("wr_sb", [128, 64], F32, offset=2368)
    br_sb = A("br_sb", [8, 1], F32, offset=2624)
    b1_sb = A("b1_sb", [128, 64], F32, offset=2656)
    b2_sb = A("b2_sb", [128, 16], F32, offset=2912)
    rs32 = A("rs32", [128, NC * T], F32, offset=4096)
    w2buf = A("w2buf", [128, 32768], BF16, offset=4096)
    wq_sb = A("wq_sb", [128, NC * 3072], BF16, offset=69632)
    wo_sb = A("wo_sb", [128, 8192], BF16, offset=118784)
    x1bf = A("x1bf", [128, NC * T], BF16, offset=69632)
    ybf = A("ybf", [128, NC * T], BF16, offset=102400)
    csq_sb = A("csq_sb", [128, T], BF16, offset=135168)
    snq_sb = A("snq_sb", [128, T], BF16, offset=139264)
    csk_sb = A("csk_sb", [128, T], BF16, offset=143360)
    snk_sb = A("snk_sb", [128, T], BF16, offset=147456)
    masku8 = [A(f"masku8{i}", [128, T], U8, offset=135168 + 2048 * i)
              for i in range(2)]
    gatebc = A("gatebc", [128, T], BF16, offset=139264)
    srow = A("srow", [1, 2048], F32, offset=143360)
    xbf = A("xbf", [128, T], BF16, offset=151552)
    qraw = A("qraw", [128, T], BF16, offset=155648)
    kraw = A("kraw", [128, T], BF16, offset=159744)
    sraw = A("sraw", [128, T], BF16, offset=163840)
    vpadO = A("vpadO", [128, T], BF16, offset=167936)
    kpe = A("kpe", [128, T], BF16, offset=172032)
    kpo = A("kpo", [128, T], BF16, offset=176128)
    vpadE = A("vpadE", [128, T], BF16, offset=180224)
    oblk = A("oblk", [128, T], BF16, offset=184320)
    p_sb = [A(f"p_sb{i}", [128, 384], BF16, offset=188416 + 768 * i)
            for i in range(2)]
    pt_sb = [A(f"pt_sb{i}", [128, 384], BF16, offset=189952 + 768 * i)
             for i in range(2)]
    d_sb = [A(f"d_sb{i}", [128, 256], BF16, offset=191488 + 512 * i)
            for i in range(2)]
    nm_sb = A("nm_sb", [128, 4], F32, offset=192512)
    ss_sb = A("ss_sb", [128, 4], F32, offset=192544)
    rc_sb = A("rc_sb", [128, 4], F32, offset=192576)
    w1slab = [A(f"w1slab{i}", [128, 1024], BF16, offset=151552 + 2048 * i)
              for i in range(2)]
    h1stage = [A(f"h1stage{i}", [128, T], BF16, offset=155648 + 4096 * i)
               for i in range(2)]
    h1in = [A(f"h1in{i}", [128, TT], BF16,
              offset=(163840 + 1024 * i) if i < 4 else
                     (192608 + 1024 * (i - 4)))
            for i in range(8)]
    yestage = [A(f"yestage{i}", [128, TT], BF16, offset=167936 + 1024 * i)
               for i in range(4)]
    logits = A("logits", [8, T], F32, offset=172032)
    lt_tm = A("lt_tm", [128, 8], F32, offset=180224)
    lt_m = A("lt_m", [128, 8], F32, offset=180256)
    rnm = A("rnm", [128, 16], F32, offset=180352)
    rmx = A("rmx", [128, 16], F32, offset=180416)
    res_ = A("res_", [128, 16], F32, offset=180480)
    rgt = A("rgt", [128, 16], F32, offset=180544)
    rexp = A("rexp", [128, 8], F32, offset=180608)
    mrow = A("mrow", [1, T], BF16, offset=180672)
    grow = A("grow", [1, T], BF16, offset=184768)
    lnbf = [A(f"lnbf{i}", [128, TT], BF16, offset=151552 + 1024 * i)
            for i in range(2)]
    sqbf = [A(f"sqbf{i}", [128, TT], BF16, offset=153600 + 1024 * i)
            for i in range(2)]
    mbc = A("mbc", [128, TT], F32, offset=188416)
    rbc = A("rbc", [128, TT], F32, offset=190464)
    t1f = A("t1f", [128, T], F32, offset=172032)

    pb = [nc.alloc_psum_tensor(f"pb{i}", [128, 512], F32) for i in range(8)]

    P = Prog()

    for name, dst in [("ibf", c_ibf), ("maskc", c_mask), ("if32", c_if32),
                      ("onescol", c_ones_col), ("onesrow", c_ones_row),
                      ("brh", br_sb),
                      ("csq", csq_sb), ("snq", snq_sb),
                      ("csk", csk_sb), ("snk", snk_sb), ("epsv", c_eps),
                      ("onesrowf", c_ones_row_f)]:
        P.dma("sync", dst[:], d_in[name][:], inc="ldc")
    for k in range(8):
        P.dma("sync", wr_sb[:, 8 * k:8 * k + 8],
              d_in["wr"][128 * k:128 * (k + 1), :], inc="ldc")
    LDC_ALL = 16 * 20
    for c in range(NC):
        P.dma("sync", rs32[:, T * c:T * (c + 1)],
              d_in["x"][128 * c:128 * (c + 1), :], inc="ldx")
    for fc in range(NC):
        P.dma("gpsimd", wq_sb[:, 3072 * fc:3072 * (fc + 1)],
              d_in["wqkvt"][128 * fc:128 * (fc + 1), :], inc="ldw")
    for dc in range(NC):
        P.dma("gpsimd", wo_sb[:, 1024 * dc:1024 * (dc + 1)],
              d_in["woutt"][128 * dc:128 * (dc + 1), :], inc="ldw")
    LDW_ALL = 16 * 16
    P.emit("gpsimd", lambda e: e.memset(kpe[64:128, :], 0.0), inc="gz")
    P.emit("gpsimd", lambda e: e.memset(kpo[0:64, :], 0.0), inc="gz")
    P.emit("gpsimd", lambda e: e.memset(
        vpadE[:].rearrange("p (g c) -> p g c", g=16)[:, :, 64:128], 0.0),
        inc="gz")
    P.emit("gpsimd", lambda e: e.memset(
        vpadO[:].rearrange("p (g c) -> p g c", g=16)[:, :, 0:64], 0.0),
        inc="gz")
    GZ_PADS = 4

    def rope(src, cs, sn, blk, evac_wait, is_k):
        P.dma("sync", sraw[0:32, :], src[32:64, :],
              waits=[evac_wait, ("rope", P.n("rope"))], inc="swp")
        P.dma("sync", sraw[32:64, :], src[0:32, :], inc="swp")
        P.dma("sync", sraw[64:96, :], src[96:128, :], inc="swp")
        P.dma("sync", sraw[96:128, :], src[64:96, :], inc="swp")

        def tbl(t_):
            return t_[:, BS * blk:BS * (blk + 1)].unsqueeze(1) \
                .broadcast_to([128, 8, BS])

        def r3(ap):
            return ap.rearrange("p (c t) -> p c t", c=8)
        P.emit("vector", lambda e, s=src, t=tbl(cs):
               e.tensor_tensor(r3(s[:]), r3(s[:]), t, ALU.mult),
               waits=[("swp", P.n("swp")), ("ldc", LDC_ALL),
                      ("sqk", 16 * blk), evac_wait], inc="vself")
        P.emit("vector", lambda e, t=tbl(sn):
               e.tensor_tensor(r3(sraw[:]), r3(sraw[:]), t, ALU.mult),
               inc="vself")
        vw = ("vself", P.n("vself"))
        if not is_k:
            P.emit("vector", lambda e, s=src:
                   e.tensor_add(s[:], s[:], sraw[:]),
                   waits=[vw], inc="rope")
        else:
            P.emit("vector", lambda e, s=src:
                   e.tensor_add(kpe[0:64, :], s[0:64, :], sraw[0:64, :]),
                   waits=[vw, ("gz", GZ_PADS)])
            P.emit("vector", lambda e, s=src:
                   e.tensor_add(kpo[64:128, :], s[64:128, :],
                                sraw[64:128, :]), inc="rope")

    for blk in range(NB):
        xc_waits = [("ldx", 16 * NC), ("qkmm", P.n("qkmm"))]
        for c in range(NC):
            P.emit("vector", lambda e, c=c, blk=blk: e.tensor_copy(
                xbf[:, BS * c:BS * (c + 1)],
                rs32[:, T * c + BS * blk:T * c + BS * (blk + 1)]),
                waits=xc_waits if c == 0 else [],
                inc="xc" if c == NC - 1 else None)
        qke0 = P.n("qke")
        for g in range(20):
            b = g % 8
            waits = [("xc", P.n("xc")), ("ldw", LDW_ALL),
                     ("r1add", P.n("r1add"))]
            if P.n("qke") - qke0 >= 8:
                waits.append(("qke", P.n("qke") - 7))
            if g < 16:
                jc = g
                for fc in range(NC):
                    P.emit("tensor", lambda e, b=b, jc=jc, fc=fc:
                           e.matmul(pb[b][:, 0:256],
                                    wq_sb[:, 3072 * fc + 128 * jc:
                                          3072 * fc + 128 * (jc + 1)],
                                    xbf[:, BS * fc:BS * (fc + 1)],
                                    start=(fc == 0), stop=(fc == NC - 1)),
                           waits=waits if fc == 0 else [],
                           inc="qkmm" if fc == NC - 1 else None)
                dst, cc = (qraw, jc) if jc < 8 else (kraw, jc - 8)
                P.emit("scalar", lambda e, b=b, dst=dst, cc=cc:
                       e.copy(dst[:, BS * cc:BS * (cc + 1)], pb[b][:, 0:256]),
                       waits=[("qkmm", P.n("qkmm"))], inc="qke")
            else:
                th, jg = (g - 16) // 2, (g - 16) % 2
                for fc in range(NC):
                    P.emit("tensor", lambda e, b=b, th=th, jg=jg, fc=fc:
                           e.matmul(pb[b][:, 0:512],
                                    xbf[:, BS * fc + 128 * th:
                                        BS * fc + 128 * (th + 1)],
                                    wq_sb[:, 3072 * fc + 2048 + 512 * jg:
                                          3072 * fc + 2048 + 512 * (jg + 1)],
                                    start=(fc == 0), stop=(fc == NC - 1)),
                           waits=waits if fc == 0 else [],
                           inc="qkmm" if fc == NC - 1 else None)
                for hh in range(8):
                    h_abs = 8 * jg + hh
                    vdst = vpadE if h_abs % 2 == 0 else vpadO
                    doff = 1024 * th + (h_abs // 2) * 128 + 64 * (h_abs % 2)
                    P.emit("scalar", lambda e, b=b, vdst=vdst, doff=doff,
                           hh=hh:
                           e.copy(vdst[:, doff:doff + 64],
                                  pb[b][:, 64 * hh:64 * (hh + 1)]),
                           waits=[("qkmm", P.n("qkmm"))] if hh == 0 else [],
                           inc="qke" if hh == 7 else None)
        qke_all = P.n("qke")
        rope(qraw, csq_sb, snq_sb, blk, ("qke", qke0 + 8), is_k=False)
        rope(kraw, csk_sb, snk_sb, blk, ("qke", qke0 + 16), is_k=True)

        B = {s: P.n(s) for s in ("sqk", "smax", "sexp", "rcp", "srcp",
                                 "ptmm", "ptea", "avmm", "oevac",
                                 "rope")}
        ROPE2 = B["rope"]

        def s_mm(h):
            par = h % 2
            c = h // 2
            kp = kpe if h % 2 == 0 else kpo
            sbank = pb[par]
            w0 = [("rope", ROPE2), ("gz", GZ_PADS),
                  ("ldc", LDC_ALL), ("qke", qke0 + 18),
                  ("sexp", B["sexp"] + h - 1)]
            P.emit("tensor", lambda e, sbank=sbank: e.matmul(
                sbank[:, 0:384], c_ibf[:], c_mask[:],
                start=True, stop=False), waits=w0)
            P.emit("tensor", lambda e, sbank=sbank, c=c, kp=kp: e.matmul(
                sbank[:, 0:128], qraw[:, BS * c:BS * c + 128],
                kp[:, BS * c:BS * c + 128], start=False, stop=False))
            P.emit("tensor", lambda e, sbank=sbank, c=c, kp=kp: e.matmul(
                sbank[:, 128:384], qraw[:, BS * c + 128:BS * c + 256],
                kp[:, BS * c:BS * (c + 1)], start=False, stop=True),
                inc="sqk")

        def pt_mm(h):
            par = h % 2
            ptb = pb[2 + par]
            wp = [("srcp", B["srcp"] + h + 1),
                  ("ptea", B["ptea"] + h - 1)]
            if h < 2:
                wp.append(("qke", qke0 + 20))
            P.emit("tensor", lambda e, ptb=ptb, par=par: e.matmul(
                ptb[:, 0:128], p_sb[par][:, 0:128], d_sb[par][:, 0:128],
                start=True, stop=True), waits=wp)
            P.emit("tensor", lambda e, ptb=ptb, par=par: e.matmul(
                ptb[:, 128:256], p_sb[par][:, 128:256],
                d_sb[par][:, 128:256], start=True, stop=True))
            P.emit("tensor", lambda e, ptb=ptb, par=par: e.matmul(
                ptb[:, 256:384], p_sb[par][:, 256:384],
                d_sb[par][:, 128:256], start=True, stop=True), inc="ptmm")

        def av_mm(c):
            ob = pb[4 + c % 2]
            wa = [("ptea", B["ptea"] + 2 * c + 2),
                  ("oevac", B["oevac"] + c - 1)]
            if c == 0:
                wa.append(("qke", qke_all))
            P.emit("tensor", lambda e, ob=ob, c=c: e.matmul(
                ob[:, 0:256], vpadE[:, 128 * c:128 * (c + 1)],
                pt_sb[0][:, 0:256], start=True, stop=False), waits=wa)
            P.emit("tensor", lambda e, ob=ob, c=c: e.matmul(
                ob[:, 0:256], vpadO[:, 128 * c:128 * (c + 1)],
                pt_sb[1][:, 0:256], start=False, stop=False))
            P.emit("tensor", lambda e, ob=ob, c=c: e.matmul(
                ob[:, 128:256], vpadE[:, 1024 + 128 * c:1024 + 128 * (c + 1)],
                pt_sb[0][:, 256:384], start=False, stop=False))
            P.emit("tensor", lambda e, ob=ob, c=c: e.matmul(
                ob[:, 128:256], vpadO[:, 1024 + 128 * c:1024 + 128 * (c + 1)],
                pt_sb[1][:, 256:384], start=False, stop=True), inc="avmm")

        def dve_max(h):
            par = h % 2
            sbank = pb[par]
            P.emit("vector", lambda e, sbank=sbank, par=par: e.reduce_max(
                nm_sb[:, 2 * par:2 * par + 1], sbank[:, 0:128], axis=AX.X,
                negate=True),
                waits=[("sqk", B["sqk"] + h + 1),
                       ("sexp", B["sexp"] + h - 1)])
            P.emit("vector", lambda e, sbank=sbank, par=par: e.reduce_max(
                nm_sb[:, 2 * par + 1:2 * par + 2], sbank[:, 128:384],
                axis=AX.X, negate=True), inc="smax")

        def dve_recip(h):
            par = h % 2
            P.emit("vector", lambda e, par=par: e.reciprocal(
                rc_sb[:, 2 * par:2 * par + 2], ss_sb[:, 2 * par:2 * par + 2]),
                waits=[("sexp", B["sexp"] + h + 1)], inc="rcp")

        def pool_d(h, eng="gpsimd"):
            par = h % 2
            P.emit(eng, lambda e, par=par: e.tensor_scalar(
                d_sb[par][:, 0:128], c_ibf[:], rc_sb[:, 2 * par:2 * par + 1],
                None, ALU.mult),
                waits=[("rcp", B["rcp"] + h + 1),
                       ("ptmm", B["ptmm"] + h - 1)])
            P.emit(eng, lambda e, par=par: e.tensor_scalar(
                d_sb[par][:, 128:256], c_ibf[:],
                rc_sb[:, 2 * par + 1:2 * par + 2], None, ALU.mult),
                inc="srcp")

        def act_exp(h):
            par = h % 2
            sbank = pb[par]
            we = [("smax", B["smax"] + h + 1),
                  ("ptmm", B["ptmm"] + h - 1),
                  ("rcp", B["rcp"] + h - 1)]
            P.emit("scalar", lambda e, sbank=sbank, par=par: e.activation(
                p_sb[par][:, 0:128], sbank[:, 0:128], AF.Exp,
                bias=nm_sb[:, 2 * par:2 * par + 1],
                accum_out=ss_sb[:, 2 * par:2 * par + 1]), waits=we)
            P.emit("scalar", lambda e, sbank=sbank, par=par: e.activation(
                p_sb[par][:, 128:384], sbank[:, 128:384], AF.Exp,
                bias=nm_sb[:, 2 * par + 1:2 * par + 2],
                accum_out=ss_sb[:, 2 * par + 1:2 * par + 2]), inc="sexp")

        def pte(h):
            par = h % 2
            ptb = pb[2 + par]
            P.emit("scalar", lambda e, ptb=ptb, par=par: e.copy(
                pt_sb[par][:, 0:384], ptb[:, 0:384]),
                waits=[("ptmm", B["ptmm"] + h + 1),
                       ("avmm", B["avmm"] + h // 2)], inc="ptea")

        def o_evac(c):
            ob = pb[4 + c % 2]
            P.emit("scalar", lambda e, ob=ob, c=c: e.copy(
                oblk[:, BS * c:BS * (c + 1)], ob[:, 0:256]),
                waits=[("avmm", B["avmm"] + c + 1)], inc="oevac")

        if LOOKAHEAD:
            s_mm(0)
            s_mm(1)
            for h in range(NH):
                if h + 2 < NH:
                    s_mm(h + 2)
                pt_mm(h)
                if h % 2 == 1:
                    av_mm(h // 2)
        else:
            for h in range(NH):
                s_mm(h)
                pt_mm(h)
                if h % 2 == 1:
                    av_mm(h // 2)
        dve_max(0)
        dve_max(1)
        for h in range(NH):
            dve_recip(h)
            pool_d(h, eng="vector")
            if h + 2 < NH:
                dve_max(h + 2)
        for h in range(NH):
            act_exp(h)
            if h >= 1:
                pte(h - 1)
                if (h - 1) % 2 == 1:
                    o_evac((h - 1) // 2)
        pte(NH - 1)
        o_evac(NH // 2 - 1)

        for jc in range(NC):
            b = jc % 8
            ww = [("oevac", P.n("oevac")), ("ldw", LDW_ALL)]
            for dc in range(NC):
                P.emit("tensor", lambda e, b=b, jc=jc, dc=dc: e.matmul(
                    pb[b][:, 0:256],
                    wo_sb[:, 1024 * dc + 128 * jc:1024 * dc + 128 * (jc + 1)],
                    oblk[:, BS * dc:BS * (dc + 1)],
                    start=(dc == 0), stop=(dc == NC - 1)),
                    waits=ww if dc == 0 else [],
                    inc="womm" if dc == NC - 1 else None)
            P.emit("vector", lambda e, b=b, jc=jc, blk=blk: e.tensor_add(
                rs32[:, T * jc + BS * blk:T * jc + BS * (blk + 1)],
                rs32[:, T * jc + BS * blk:T * jc + BS * (blk + 1)],
                pb[b][:, 0:256]),
                waits=[("womm", P.n("womm"))], inc="r1add")

    if stage == "attn":
        P.dma("sync", dbg["rs"][:], rs32[:],
              waits=[("r1add", P.n("r1add"))], inc="sto")
        P.emit("gpsimd", lambda e: e.engine_nop(),
               waits=[("sto", P.n("sto"))])
        return nc, P, dbg

    def layer_norm(gate, emit_x1bf, emit_spill):
        for tg in range(NTT):
            gcols = slice(TT * tg, TT * (tg + 1))
            for c in range(NC):
                sl = c % 2
                wv = [gate(tg) if callable(gate) else gate]
                if P.n("lnpe") >= 2:
                    wv.append(("lnpe", P.n("lnpe") - 1))
                if P.n("lnsq") >= 2:
                    wv.append(("lnsq", P.n("lnsq") - 1))
                P.emit("vector", lambda e, c=c, sl=sl, g=gcols: e.tensor_copy(
                    lnbf[sl][:], rs32[:, T * c:T * (c + 1)][:, g]),
                    waits=wv, inc="lncast")
                sqw = [("lncast", P.n("lncast"))]
                if P.n("lnpe") >= 2:
                    sqw.append(("lnpe", P.n("lnpe") - 1))
                P.emit("scalar", lambda e, sl=sl: e.activation(
                    sqbf[sl][:], lnbf[sl][:], AF.Square),
                    waits=sqw, inc="lnsq")
                mmw = [("lncast", P.n("lncast")), ("ldc", LDC_ALL)]
                if c == 0:
                    mmw += [("lnbce", P.n("lnbce")), ("lnsev", P.n("lnsev"))]
                P.emit("tensor", lambda e, sl=sl, c=c: e.matmul(
                    pb[6][0:1, 0:TT], c_ones_col[:], lnbf[sl][:],
                    start=(c == 0), stop=(c == NC - 1)), waits=mmw)
                P.emit("tensor", lambda e, sl=sl, c=c: e.matmul(
                    pb[7][0:1, 0:TT], c_ones_col[:], sqbf[sl][:],
                    start=(c == 0), stop=(c == NC - 1)),
                    waits=[("lnsq", P.n("lnsq"))], inc="lnpe")
            sA = srow[0:1, 0:512]
            sB = srow[0:1, 512:1024]
            sM = srow[0:1, 1024:1536]
            sV = srow[0:1, 1536:2048]
            P.emit("vector", lambda e: e.tensor_copy(sA, pb[6][0:1, 0:TT]),
                   waits=[("lnpe", P.n("lnpe"))], inc="vself")
            P.emit("vector", lambda e: e.tensor_copy(sB, pb[7][0:1, 0:TT]),
                   inc="lnsev")
            P.emit("vector", lambda e: e.tensor_scalar_mul(sM, sA, 1.0 / D),
                   waits=[("vself", P.n("vself")), ("lnsev", P.n("lnsev")),
                          ("lnbcm", P.n("lnbcm"))], inc="vself")
            P.emit("scalar", lambda e: e.activation(sV, sM, AF.Square),
                   waits=[("vself", P.n("vself")),
                          ("lnbcm", P.n("lnbcm"))], inc="aself")
            P.emit("vector", lambda e: e.scalar_tensor_tensor(
                sV, sB, 1.0 / D, sV, ALU.mult, ALU.subtract),
                waits=[("aself", P.n("aself")), ("lnsev", P.n("lnsev"))],
                inc="lnst")
            P.emit("scalar", lambda e: e.activation(sV, sV, AF.Ln, bias=c_eps[0:1, 0:1]),
                   waits=[("lnst", P.n("lnst"))], inc="aself")
            P.emit("scalar", lambda e: e.activation(sV, sV, AF.Exp,
                                                    scale=-0.5),
                   waits=[("aself", P.n("aself"))], inc="lnrstd")
            P.emit("tensor", lambda e: e.matmul(
                pb[6][:, 0:TT], c_ones_row_f[0:1, :], sM, start=True,
                stop=True), waits=[("lnrstd", P.n("lnrstd"))])
            P.emit("tensor", lambda e: e.matmul(
                pb[7][:, 0:TT], c_ones_row_f[0:1, :], sV, start=True,
                stop=True), inc="lnbcm")
            P.emit("vector", lambda e: e.tensor_copy(mbc[:], pb[6][:, 0:TT]),
                   waits=[("lnbcm", P.n("lnbcm"))], inc="vself")
            P.emit("vector", lambda e: e.tensor_copy(rbc[:], pb[7][:, 0:TT]),
                   inc="lnbce")
            for c in range(NC):
                view = rs32[:, T * c:T * (c + 1)][:, gcols]
                P.emit("vector", lambda e, v=view: e.tensor_sub(v, v, mbc[:]),
                       waits=[("lnbce", P.n("lnbce")),
                              ("vself", P.n("vself"))], inc="vself")
                P.emit("vector", lambda e, v=view: e.tensor_mul(v, v, rbc[:]),
                       waits=[("vself", P.n("vself"))], inc="lnnorm")
                if emit_x1bf:
                    P.emit("vector", lambda e, v=view, c=c, g=gcols:
                           e.tensor_copy(x1bf[:, T * c:T * (c + 1)][:, g], v),
                           waits=[("lnnorm", P.n("lnnorm"))], inc="x1c")
        if emit_spill:
            for c in range(NC):
                P.dma("sync", x1d[c, :, :], rs32[:, T * c:T * (c + 1)],
                      waits=[("lnnorm", P.n("lnnorm"))], inc="spx1")

    layer_norm(("r1add", P.n("r1add")), emit_x1bf=True, emit_spill=True)
    LNNORM1 = P.n("lnnorm")
    X1C_ALL = P.n("x1c")

    for tg in range(NTT):
        gcols = slice(TT * tg, TT * (tg + 1))
        rw = [("lnnorm", LNNORM1), ("ldc", LDC_ALL), ("rtev", P.n("rtev"))]
        for k in range(8):
            P.emit("tensor", lambda e, k=k, g=gcols: e.matmul(
                pb[4][0:8, 0:TT], wr_sb[:, 8 * k:8 * (k + 1)],
                rs32[:, T * k:T * (k + 1)][:, g],
                start=(k == 0), stop=(k == 7)),
                waits=rw if k == 0 else [],
                inc="rtmm" if k == 7 else None)
        P.emit("scalar", lambda e, g=gcols: e.activation(
            logits[0:8, g], pb[4][0:8, 0:TT], AF.Identity, bias=br_sb[0:8, 0:1]),
            waits=[("rtmm", P.n("rtmm"))], inc="rtev")
    RTEV_ALL = P.n("rtev")
    for cc in range(16):
        P.emit("tensor", lambda e, cc=cc: e.transpose(
            pb[5][:, 0:8], logits[0:8, 128 * cc:128 * (cc + 1)],
            c_if32[0:8, 0:8]),
            waits=[("rtev", RTEV_ALL), ("rtc", P.n("rtc"))], inc="rtc")
        P.emit("vector", lambda e: e.tensor_copy(lt_tm[:], pb[5][:, 0:8]),
               waits=[("rtc", P.n("rtc"))], inc="rtc")
        P.emit("vector", lambda e, cc=cc: e.reduce_max(
            rnm[:, cc:cc + 1], lt_tm[:], axis=AX.X, negate=True),
            waits=[("rtc", P.n("rtc"))], inc="rtc")
        P.emit("scalar", lambda e, cc=cc: e.activation(
            rexp[:], lt_tm[:], AF.Exp, bias=rnm[:, cc:cc + 1],
            accum_out=res_[:, cc:cc + 1]),
            waits=[("rtc", P.n("rtc"))], inc="rtc")
        P.emit("vector", lambda e, cc=cc: e.reciprocal(
            rgt[:, cc:cc + 1], res_[:, cc:cc + 1]),
            waits=[("rtc", P.n("rtc"))], inc="rtc")
        P.emit("vector", lambda e, cc=cc: e.tensor_scalar_mul(
            rmx[:, cc:cc + 1], rnm[:, cc:cc + 1], -1.0),
            waits=[("rtc", P.n("rtc"))], inc="rtc")
        P.emit("vector", lambda e, cc=cc: e.tensor_scalar(
            lt_m[:], lt_tm[:], rmx[:, cc:cc + 1], None, ALU.is_equal),
            waits=[("rtc", P.n("rtc"))], inc="rtc")
        P.emit("tensor", lambda e, cc=cc: e.transpose(
            pb[5][0:8, 8:136], lt_m[:], c_if32[:]),
            waits=[("rtc", P.n("rtc"))], inc="rtc")
        P.emit("vector", lambda e, cc=cc: e.tensor_copy(
            logits[0:8, 128 * cc:128 * (cc + 1)], pb[5][0:8, 8:136]),
            waits=[("rtc", P.n("rtc"))], inc="rtc")
        P.emit("tensor", lambda e, cc=cc: e.transpose(
            pb[5][0:1, 136:264], rgt[:, cc:cc + 1], c_if32[:]),
            waits=[("rtc", P.n("rtc"))], inc="rtc")
        P.emit("vector", lambda e, cc=cc: e.tensor_copy(
            grow[0:1, 128 * cc:128 * (cc + 1)], pb[5][0:1, 136:264]),
            waits=[("rtc", P.n("rtc"))], inc="rtc")
    RTC_ALL = P.n("rtc")
    for tg in range(NTT):
        P.emit("tensor", lambda e, tg=tg: e.matmul(
            pb[5][:, 0:TT], c_ones_row[0:1, :],
            grow[0:1, TT * tg:TT * (tg + 1)], start=True, stop=True),
            waits=[("rtc", RTC_ALL), ("gbce", P.n("gbce"))], inc="gbcm")
        P.emit("vector", lambda e, tg=tg: e.tensor_copy(
            gatebc[:, TT * tg:TT * (tg + 1)], pb[5][:, 0:TT]),
            waits=[("gbcm", P.n("gbcm"))], inc="gbce")
    GBCE_ALL = P.n("gbce")

    if stage == "ln1":
        P.dma("sync", dbg["rs"][:], rs32[:],
              waits=[("lnnorm", LNNORM1)], inc="sto")
        P.dma("gpsimd", dbg["aux"][:, 0:T], gatebc[:],
              waits=[("gbce", GBCE_ALL)], inc="sto")
        P.dma("sync", dbg["aux"][0:8, T:2 * T], logits[0:8, :],
              waits=[("rtc", RTC_ALL)], inc="sto")
        P.dma("sync", dbg["aux"][:, 2 * T:2 * T + TT], mbc[:],
              waits=[("lnnorm", LNNORM1)], inc="sto")
        P.dma("sync", dbg["aux"][:, 2 * T + TT:2 * T + 2 * TT], rbc[:],
              waits=[("lnnorm", LNNORM1)], inc="sto")
        P.dma("sync", dbg["aux"][0:1, 2 * T + 2 * TT:2 * T + 2 * TT + 2048],
              srow[0:1, :], waits=[("lnnorm", LNNORM1)], inc="sto")
        P.emit("gpsimd", lambda e: e.engine_nop(),
               waits=[("sto", P.n("sto"))])
        return nc, P, dbg

    P.emit("gpsimd", lambda e: e.memset(ybf[:], 0.0),
           waits=[("r1add", P.n("r1add"))], inc="ybz")

    for e_ in range(E):
        par = e_ % 2
        P.dma("sync", b1_sb[:, 32 * par:32 * (par + 1)],
              d_in["b1"][e_, :, :],
              waits=[("h1ev", max(0, P.n("h1ev") - 128))], inc=f"b1ld{par}")
        P.dma("sync", b2_sb[:, 8 * par:8 * (par + 1)],
              d_in["b2"][e_, :, :],
              waits=[("yeev", max(0, P.n("yeev") - 32))], inc=f"b2ld{par}")
        P.dma("gpsimd", mrow[0:1, :], logits[e_:e_ + 1, :],
              waits=[("rtc", RTC_ALL), ("mbcm", P.n("mbcm"))], inc="mrld")
        for tg in range(NTT):
            bnk = 6 + tg % 2
            P.emit("tensor", lambda e, tg=tg, bnk=bnk: e.matmul(
                pb[bnk][:, 0:TT], c_ones_row[0:1, :],
                mrow[0:1, TT * tg:TT * (tg + 1)], start=True, stop=True),
                waits=[("mrld", P.n("mrld")), ("lnbce", P.n("lnbce")),
                       ("mbce", max(0, P.n("mbce") - 1)),
                       ("h1ev", P.n("h1ev")), ("yeev", P.n("yeev"))],
                inc="mbcm")
            P.emit("vector", lambda e, tg=tg, par=par, bnk=bnk: e.tensor_copy(
                masku8[par][:, TT * tg:TT * (tg + 1)], pb[bnk][:, 0:TT]),
                waits=[("mbcm", P.n("mbcm")),
                       ("comb", max(0, P.n("comb") - 32))], inc="mbce")
        spill_counts = {}
        hld_wait = {}
        for fo in range(NFO):
            sl = fo % 2
            P.dma("gpsimd", w1slab[sl][:], d_in["w1"][e_, fo, :, :],
                  waits=[("ph1mm", max(0, P.n("ph1mm") - 1)),
                         ("lncast", P.n("lncast")),
                         ("lnpe", P.n("lnpe")), ("lnsq", P.n("lnsq"))],
                  inc=f"w1ld{sl}")
            if fo < 16:
                if e_ == 0:
                    w2w = ([("rtmm", P.n("rtmm")), ("spx1", P.n("spx1")),
                            ("lnnorm", LNNORM1), ("x1c", X1C_ALL)]
                           if fo == 0 else [])
                else:
                    w2w = [("ph2", 128 * e_)] if fo == 0 else []
                P.dma("gpsimd", w2buf[:, 2048 * fo:2048 * (fo + 1)],
                      d_in["w2"][e_, :, 2048 * fo:2048 * (fo + 1)],
                      waits=w2w, inc="w2ld")
            mw = [(f"w1ld{sl}", P.n(f"w1ld{sl}")),
                  ("h1ev", max(0, P.n("h1ev") - 4))]
            if fo < 2:
                mw += [("yeev", P.n("yeev")), ("mbce", P.n("mbce"))]
            if fo < 2 and e_ == 0:
                mw += [("lnbce", P.n("lnbce")), ("rtc", RTC_ALL),
                       ("gbce", GBCE_ALL)]
            for k in range(8):
                for g in range(4):
                    gw = list(mw) if (k == 0 and g == 0) else []
                    if fo == 0 and k == 0 and e_ == 0:
                        gw.append(("x1c", min(8 * (g + 1), X1C_ALL)))
                    P.emit("tensor", lambda e, sl=sl, k=k, g=g, fo=fo:
                           e.matmul(pb[4 * (fo % 2) + g][:, 0:TT],
                                    w1slab[sl][:, 128 * k:128 * (k + 1)],
                                    x1bf[:, T * k + TT * g:T * k + TT * (g + 1)],
                                    start=(k == 0), stop=(k == 7)),
                           waits=gw,
                           inc="ph1mm" if (k == 7 and g == 3) else None)
            for g in range(4):
                P.emit("scalar", lambda e, sl=sl, g=g, fo=fo, par=par:
                       e.activation(h1stage[sl][:, TT * g:TT * (g + 1)],
                                    pb[4 * (fo % 2) + g][:, 0:TT], AF.Relu,
                                    bias=b1_sb[:, 32 * par + fo:
                                               32 * par + fo + 1]),
                       waits=[("ph1mm", P.n("ph1mm")),
                              (f"b1ld{par}", P.n(f"b1ld{par}")),
                              (f"h1sp{sl}", P.n(f"h1sp{sl}"))]
                       if g == 0 else [],
                       inc="h1ev")
            P.emit("sync", lambda e, sl=sl, fo=fo: e.dma_start(
                out=h1d[:, fo, :, :].rearrange("g p t -> p g t"),
                in_=h1stage[sl][:].rearrange("p (g t) -> p g t", g=4)),
                waits=[("h1ev", P.n("h1ev"))], inc=f"h1sp{sl}", inck=16)
            spill_counts[fo] = P.n(f"h1sp{sl}")

        for fo in range(8):
            P.dma("gpsimd", h1in[fo][:], h1d[0, fo, :, :],
                  waits=[(f"h1sp{fo % 2}", spill_counts[fo]),
                         ("ph2", max(0, P.n("ph2") - 7 + fo))],
                  inc=f"h1lp{fo}")
            hld_wait[fo] = (f"h1lp{fo}", P.n(f"h1lp{fo}"))
        for tt in range(NTT):
            for fo in range(NFO):
                idx = tt * NFO + fo
                buf = idx % 8
                if idx >= 8:
                    P.dma("sync", h1in[buf][:], h1d[tt, fo, :, :],
                          waits=[(f"h1sp{fo % 2}", spill_counts[fo]),
                                 ("ph2", max(0, P.n("ph2") - 7))],
                          inc=f"h1ld{buf}")
                    hld_wait[idx] = (f"h1ld{buf}", P.n(f"h1ld{buf}"))
                for h in range(8):
                    mw2 = [hld_wait[idx]]
                    if fo == 0:
                        mw2.append(("w2ld", P.n("w2ld")))
                        if tt == 0:
                            mw2 += [("h1ev", P.n("h1ev")),
                                    ("mbce", P.n("mbce"))]
                        else:
                            mw2.append(("yeev", P.n("yeev") - 7 + h))
                    P.emit("tensor", lambda e, h=h, fo=fo, buf=buf: e.matmul(
                        pb[h][:, 0:TT],
                        w2buf[:, 128 * (8 * fo + h):128 * (8 * fo + h + 1)],
                        h1in[buf][:], start=(fo == 0), stop=(fo == NFO - 1)),
                        waits=mw2,
                        inc="ph2" if h == 7 else None)
            for h in range(8):
                sbuf_ = yestage[h % 4]
                P.emit("scalar", lambda e, h=h, par=par, sbuf_=sbuf_:
                       e.activation(sbuf_[:], pb[h][:, 0:TT], AF.Identity,
                                    bias=b2_sb[:, 8 * par + h:
                                               8 * par + h + 1]),
                       waits=[("ph2", P.n("ph2")),
                              (f"b2ld{par}", P.n(f"b2ld{par}")),
                              ("comb", max(0, P.n("comb") - 3))],
                       inc="yeev")
                P.emit("vector", lambda e, h=h, tt=tt, par=par, sbuf_=sbuf_:
                       e.copy_predicated(
                           ybf[:, T * h + TT * tt:T * h + TT * (tt + 1)],
                           masku8[par][:, TT * tt:TT * (tt + 1)], sbuf_[:]),
                       waits=[("yeev", P.n("yeev")), ("ybz", 1),
                              ("mbce", P.n("mbce"))], inc="comb")

    COMB_ALL = P.n("comb")
    for c in range(NC):
        P.dma("sync", rs32[:, T * c:T * (c + 1)], x1d[c, :, :],
              waits=[("ph2", P.n("ph2")), ("comb", COMB_ALL)]
              if c == 0 else [], inc="x1ld")
    X1LD_ALL = P.n("x1ld")
    for c in range(NC):
        P.emit("vector", lambda e, c=c: e.tensor_tensor(
            ybf[:, T * c:T * (c + 1)], ybf[:, T * c:T * (c + 1)],
            gatebc[:], ALU.mult),
            waits=[("comb", COMB_ALL), ("gbce", GBCE_ALL)]
            if c == 0 else [], inc="tailmul")
    TMUL = P.n("tailmul")
    for tg in range(NTT):
        for c in range(NC):
            sl_ = slice(T * c + TT * tg, T * c + TT * (tg + 1))
            P.emit("vector", lambda e, s=sl_: e.tensor_add(
                rs32[:, s], rs32[:, s], ybf[:, s]),
                waits=[("x1ld", X1LD_ALL), ("tailmul", TMUL)]
                if (tg == 0 and c == 0) else [], inc="tailadd")
    TA0 = P.n("tailadd") - 4 * NC
    layer_norm(lambda tg: ("tailadd", TA0 + NC * (tg + 1)),
               emit_x1bf=False, emit_spill=False)
    for c in range(NC):
        P.dma("sync", y_d[128 * c:128 * (c + 1), :],
              rs32[:, T * c:T * (c + 1)],
              waits=[("lnnorm", P.n("lnnorm"))] if c == 0 else [], inc="sto")
    P.emit("gpsimd", lambda e: e.engine_nop(),
           waits=[("sto", P.n("sto"))])
    return nc, P, dbg


_CACHED = {}


def _get_nc(stage="full"):
    if stage not in _CACHED:
        nc, P, dbg = build_program(stage)
        P.realize(nc)
        _CACHED[stage] = nc
    return _CACHED[stage]


def kernel(**inputs):
    maps = prep_inputs(**inputs)
    nc = _get_nc("full")
    res = run_bass_kernel_spmd(nc, maps, core_ids=list(range(8)))
    outs = []
    for r in res.results:
        y = np.asarray(r["y"])
        outs.append(y.T)
    return np.ascontiguousarray(np.stack(outs, axis=0))


# version 09
# speedup vs baseline: 1.1822x; 1.0040x over previous
import math
import contextlib
import numpy as np
import ml_dtypes
import concourse.bass as bass
import concourse.mybir as mybir
from concourse.bass_utils import run_bass_kernel_spmd

F32 = mybir.dt.float32
BF16 = mybir.dt.bfloat16
U8 = mybir.dt.uint8
AF = mybir.ActivationFunctionType
ALU = mybir.AluOpType
AX = mybir.AxisListType

D = 1024
T = 2048
NB = 8
BS = 256
NH = 16
E = 8
DFF = 4096
NFO = DFF // 128
NC = D // 128
TT = 512
NTT = T // TT
EPS = 1e-5
MASKVAL = -1e9

ENGINES = ("sync", "gpsimd", "vector", "scalar", "tensor")
LOOKAHEAD = 2


class Prog:

    def __init__(self):
        self.ops = {e: [] for e in ENGINES}
        self.cnt = {}

    def n(self, sem):
        return self.cnt.get(sem, 0)

    def emit(self, eng, fn, waits=(), inc=None, inck=1):
        ws = [(s, v) for (s, v) in waits if v > 0]
        rec = (fn, ws, (inc, inck) if inc is not None else None)
        if inc is not None:
            self.cnt[inc] = self.cnt.get(inc, 0) + inck
        self.ops[eng].append(rec)

    def dma(self, eng, out, in_, waits=(), inc=None):
        self.emit(eng, lambda e, o=out, i=in_: e.dma_start(out=o, in_=i),
                  waits=waits, inc=inc, inck=16)

    def realize(self, nc):
        names = set(self.cnt.keys())
        for e in ENGINES:
            for (_, ws, _) in self.ops[e]:
                for (s, _) in ws:
                    names.add(s)
        with contextlib.ExitStack() as st:
            hs = {s: st.enter_context(nc.semaphore(s)) for s in sorted(names)}
            with nc.Block() as blk:
                for eng in ENGINES:
                    recs = self.ops[eng]
                    if not recs:
                        continue

                    def make(recs):
                        def body(e):
                            for fn, ws, inc in recs:
                                for (s, v) in ws:
                                    e.wait_ge(hs[s], v)
                                inst = fn(e)
                                if inc is not None:
                                    inst.then_inc(hs[inc[0]], inc[1])
                        return body
                    getattr(blk, eng)(make(recs))



def _rope_tables():
    inv = 10000.0 ** (-np.arange(0, 64, 2, dtype=np.float64) / 64)
    t = np.arange(T, dtype=np.float64)
    ang = t[None, :] * inv[:, None]
    C = np.cos(ang)
    S = np.sin(ang)
    CS = np.tile(C, (4, 1))
    SN2 = np.concatenate([-S, S, -S, S], axis=0)
    sc = 0.125
    b = ml_dtypes.bfloat16
    return (np.ascontiguousarray((CS * sc).astype(b)),
            np.ascontiguousarray((SN2 * sc).astype(b)),
            np.ascontiguousarray(CS.astype(b)),
            np.ascontiguousarray(SN2.astype(b)))


def _perm_qk():
    p = []
    for c in range(NC):
        for h in (2 * c, 2 * c + 1):
            p += [64 * h + 2 * i for i in range(32)]
            p += [64 * h + 2 * i + 1 for i in range(32)]
    return np.array(p)


def prep_inputs(L0_attn_x, L0_attn_Wqkv, L0_attn_Wout, L0_ffn_Wr, L0_ffn_br,
                L0_ffn_W1, L0_ffn_b1, L0_ffn_W2, L0_ffn_b2):
    f = np.float32
    x = np.asarray(L0_attn_x, f)
    Wqkv = np.asarray(L0_attn_Wqkv, f)
    Wout = np.asarray(L0_attn_Wout, f)
    Wr = np.asarray(L0_ffn_Wr, f)
    br = np.asarray(L0_ffn_br, f)
    W1 = np.asarray(L0_ffn_W1, f)
    b1 = np.asarray(L0_ffn_b1, f)
    W2 = np.asarray(L0_ffn_W2, f)
    b2 = np.asarray(L0_ffn_b2, f)

    perm = _perm_qk()
    full_perm = np.concatenate([perm, 1024 + perm, 2048 + np.arange(1024)])
    wqkvt = np.ascontiguousarray(Wqkv[full_perm].T)
    woutt = np.ascontiguousarray(Wout.T)
    w1h = np.ascontiguousarray(
        W1.reshape(E, 8, 128, NFO, 128).transpose(0, 3, 2, 1, 4)
        .reshape(E, NFO, 128, 1024))
    b1h = np.ascontiguousarray(b1.reshape(E, NFO, 128).transpose(0, 2, 1))
    w2h = np.ascontiguousarray(
        W2.reshape(E, NFO, 128, 8, 128).transpose(0, 2, 1, 3, 4)
        .reshape(E, 128, 32768))
    b2h = np.ascontiguousarray(b2.reshape(E, 8, 128).transpose(0, 2, 1))
    brh = np.ascontiguousarray(br.reshape(8, 1))

    csq, snq, csk, snk = _rope_tables()
    bfe = ml_dtypes.bfloat16
    shared = dict(
        wqkvt=wqkvt, woutt=woutt, wr=np.ascontiguousarray(Wr), brh=brh,
        w1=w1h, b1=b1h, w2=w2h, b2=b2h,
        csq=csq, snq=snq, csk=csk, snk=snk,
        ibf=np.eye(128, dtype=bfe),
        if32=np.eye(128, dtype=f),
        maskc=np.concatenate([
            _mk := np.where(np.arange(128)[None, :] <= np.arange(128)[:, None],
                            0.0, MASKVAL).astype(bfe),
            np.zeros((128, 128), dtype=bfe), _mk], axis=1),
        onescol=np.ones((128, 1), dtype=bfe),
        epsv=np.full((1, 1), EPS, dtype=f),
        onesrowf=np.ones((1, 128), dtype=f),
        onesrow=np.ones((1, 128), dtype=bfe))
    maps = []
    for bi in range(x.shape[0]):
        m = dict(shared)
        m["x"] = np.ascontiguousarray(x[bi].T)
        maps.append(m)
    return maps



def build_program(stage="full"):
    nc = bass.Bass("TRN2", target_bir_lowering=False, debug=False)

    d_in = {}
    for name, shape, dt in [
        ("x", [D, T], F32), ("wqkvt", [D, 3 * D], F32), ("woutt", [D, D], F32),
        ("wr", [D, 8], F32), ("brh", [8, 1], F32),
        ("w1", [E, NFO, 128, 1024], F32), ("b1", [E, 128, NFO], F32),
        ("w2", [E, 128, 32768], F32), ("b2", [E, 128, 8], F32),
        ("csq", [128, T], BF16), ("snq", [128, T], BF16),
        ("csk", [128, T], BF16), ("snk", [128, T], BF16),
        ("ibf", [128, 128], BF16), ("if32", [128, 128], F32),
        ("maskc", [128, 384], BF16),
        ("onescol", [128, 1], BF16), ("onesrow", [1, 128], BF16),
        ("epsv", [1, 1], F32), ("onesrowf", [1, 128], F32),
    ]:
        d_in[name] = nc.dram_tensor(name, shape, dt, kind="ExternalInput").ap()
    y_d = nc.dram_tensor("y", [D, T], F32, kind="ExternalOutput").ap()
    h1d = nc.dram_tensor("h1d", [NTT, NFO, 128, TT], BF16, kind="Internal").ap()
    x1d = nc.dram_tensor("x1d", [NC, 128, T], F32, kind="Internal").ap()

    dbg = {}
    if stage != "full":
        dbg["rs"] = nc.dram_tensor("dbg_rs", [128, NC * T], F32,
                                   kind="ExternalOutput").ap()
        dbg["aux"] = nc.dram_tensor("dbg_aux", [128, 8192], F32,
                                    kind="ExternalOutput").ap()

    SBBASE = 24576

    def A(name, shape, dt, offset):
        return nc.alloc_sbuf_tensor_at(name, shape, dt, offset=offset + SBBASE)
    c_ibf = A("c_ibf", [128, 128], BF16, offset=0)
    c_mask = A("c_mask", [128, 384], BF16, offset=256)
    c_if32 = A("c_if32", [128, 128], F32, offset=1024)
    c_ones_col = A("c_ones_col", [128, 1], BF16, offset=1536)
    c_ones_row = A("c_ones_row", [1, 128], BF16, offset=1568)
    c_eps = A("c_eps", [1, 1], F32, offset=1824)
    c_ones_row_f = A("c_ones_row_f", [1, 128], F32, offset=1856)
    wr_sb = A("wr_sb", [128, 64], F32, offset=2368)
    br_sb = A("br_sb", [8, 1], F32, offset=2624)
    b1_sb = A("b1_sb", [128, 64], F32, offset=2656)
    b2_sb = A("b2_sb", [128, 16], F32, offset=2912)
    rs32 = A("rs32", [128, NC * T], F32, offset=4096)
    w2buf = A("w2buf", [128, 32768], BF16, offset=4096)
    wq_sb = A("wq_sb", [128, NC * 3072], BF16, offset=69632)
    wo_sb = A("wo_sb", [128, 8192], BF16, offset=118784)
    x1bf = A("x1bf", [128, NC * T], BF16, offset=69632)
    ybf = A("ybf", [128, NC * T], BF16, offset=102400)
    csq_sb = A("csq_sb", [128, T], BF16, offset=135168)
    snq_sb = A("snq_sb", [128, T], BF16, offset=139264)
    csk_sb = A("csk_sb", [128, T], BF16, offset=143360)
    snk_sb = A("snk_sb", [128, T], BF16, offset=147456)
    masku8 = [A(f"masku8{i}", [128, T], U8, offset=135168 + 2048 * i)
              for i in range(2)]
    gatebc = A("gatebc", [128, T], BF16, offset=139264)
    srow = A("srow", [1, 2048], F32, offset=143360)
    xbf = A("xbf", [128, T], BF16, offset=151552)
    qraw = A("qraw", [128, T], BF16, offset=155648)
    kraw = A("kraw", [128, T], BF16, offset=159744)
    sraw = A("sraw", [128, T], BF16, offset=163840)
    vpadO = A("vpadO", [128, T], BF16, offset=167936)
    kpe = A("kpe", [128, T], BF16, offset=172032)
    kpo = A("kpo", [128, T], BF16, offset=176128)
    vpadE = A("vpadE", [128, T], BF16, offset=180224)
    oblk = A("oblk", [128, T], BF16, offset=184320)
    p_sb = [A(f"p_sb{i}", [128, 384], BF16, offset=188416 + 768 * i)
            for i in range(2)]
    pt_sb = [A(f"pt_sb{i}", [128, 384], BF16, offset=189952 + 768 * i)
             for i in range(2)]
    d_sb = [A(f"d_sb{i}", [128, 256], BF16, offset=191488 + 512 * i)
            for i in range(2)]
    nm_sb = A("nm_sb", [128, 4], F32, offset=192512)
    ss_sb = A("ss_sb", [128, 4], F32, offset=192544)
    rc_sb = A("rc_sb", [128, 4], F32, offset=192576)
    w1slab = [A(f"w1slab{i}", [128, 1024], BF16, offset=151552 + 2048 * i)
              for i in range(2)]
    h1stage = [A(f"h1stage{i}", [128, T], BF16, offset=155648 + 4096 * i)
               for i in range(2)]
    h1in = [A(f"h1in{i}", [128, TT], BF16,
              offset=(163840 + 1024 * i) if i < 4 else
                     (192608 + 1024 * (i - 4)))
            for i in range(8)]
    yestage = [A(f"yestage{i}", [128, TT], BF16, offset=167936 + 1024 * i)
               for i in range(4)]
    logits = A("logits", [8, T], F32, offset=172032)
    lt_tm = A("lt_tm", [128, 8], F32, offset=180224)
    lt_m = A("lt_m", [128, 8], F32, offset=180256)
    rnm = A("rnm", [128, 16], F32, offset=180352)
    rmx = A("rmx", [128, 16], F32, offset=180416)
    res_ = A("res_", [128, 16], F32, offset=180480)
    rgt = A("rgt", [128, 16], F32, offset=180544)
    rexp = A("rexp", [128, 8], F32, offset=180608)
    mrow = A("mrow", [1, T], BF16, offset=180672)
    grow = A("grow", [1, T], BF16, offset=184768)
    lnbf = [A(f"lnbf{i}", [128, TT], BF16, offset=151552 + 1024 * i)
            for i in range(2)]
    sqbf = [A(f"sqbf{i}", [128, TT], BF16, offset=153600 + 1024 * i)
            for i in range(2)]
    mbc_s = [A("mbc0", [128, TT], F32, offset=188416),
             A("mbc1", [128, TT], F32, offset=196704)]
    rbc_s = [A("rbc0", [128, TT], F32, offset=190464),
             A("rbc1", [128, TT], F32, offset=198752)]
    t1f = A("t1f", [128, T], F32, offset=172032)

    pb = [nc.alloc_psum_tensor(f"pb{i}", [128, 512], F32) for i in range(8)]

    P = Prog()

    for name, dst in [("ibf", c_ibf), ("maskc", c_mask), ("if32", c_if32),
                      ("onescol", c_ones_col), ("onesrow", c_ones_row),
                      ("brh", br_sb),
                      ("csq", csq_sb), ("snq", snq_sb),
                      ("csk", csk_sb), ("snk", snk_sb), ("epsv", c_eps),
                      ("onesrowf", c_ones_row_f)]:
        P.dma("sync", dst[:], d_in[name][:], inc="ldc")
    for k in range(8):
        P.dma("sync", wr_sb[:, 8 * k:8 * k + 8],
              d_in["wr"][128 * k:128 * (k + 1), :], inc="ldc")
    LDC_ALL = 16 * 20
    for c in range(NC):
        P.dma("sync", rs32[:, T * c:T * (c + 1)],
              d_in["x"][128 * c:128 * (c + 1), :], inc="ldx")
    for fc in range(NC):
        P.dma("gpsimd", wq_sb[:, 3072 * fc:3072 * (fc + 1)],
              d_in["wqkvt"][128 * fc:128 * (fc + 1), :], inc="ldw")
    for dc in range(NC):
        P.dma("gpsimd", wo_sb[:, 1024 * dc:1024 * (dc + 1)],
              d_in["woutt"][128 * dc:128 * (dc + 1), :], inc="ldw")
    LDW_ALL = 16 * 16
    P.emit("gpsimd", lambda e: e.memset(kpe[64:128, :], 0.0), inc="gz")
    P.emit("gpsimd", lambda e: e.memset(kpo[0:64, :], 0.0), inc="gz")
    P.emit("gpsimd", lambda e: e.memset(
        vpadE[:].rearrange("p (g c) -> p g c", g=16)[:, :, 64:128], 0.0),
        inc="gz")
    P.emit("gpsimd", lambda e: e.memset(
        vpadO[:].rearrange("p (g c) -> p g c", g=16)[:, :, 0:64], 0.0),
        inc="gz")
    GZ_PADS = 4

    def rope(src, cs, sn, blk, evac_wait, is_k):
        P.dma("sync", sraw[0:32, :], src[32:64, :],
              waits=[evac_wait, ("rope", P.n("rope"))], inc="swp")
        P.dma("sync", sraw[32:64, :], src[0:32, :], inc="swp")
        P.dma("sync", sraw[64:96, :], src[96:128, :], inc="swp")
        P.dma("sync", sraw[96:128, :], src[64:96, :], inc="swp")

        def tbl(t_):
            return t_[:, BS * blk:BS * (blk + 1)].unsqueeze(1) \
                .broadcast_to([128, 8, BS])

        def r3(ap):
            return ap.rearrange("p (c t) -> p c t", c=8)
        P.emit("vector", lambda e, s=src, t=tbl(cs):
               e.tensor_tensor(r3(s[:]), r3(s[:]), t, ALU.mult),
               waits=[("swp", P.n("swp")), ("ldc", LDC_ALL),
                      ("sqk", 16 * blk), evac_wait], inc="vself")
        P.emit("vector", lambda e, t=tbl(sn):
               e.tensor_tensor(r3(sraw[:]), r3(sraw[:]), t, ALU.mult),
               inc="vself")
        vw = ("vself", P.n("vself"))
        if not is_k:
            P.emit("vector", lambda e, s=src:
                   e.tensor_add(s[:], s[:], sraw[:]),
                   waits=[vw], inc="rope")
        else:
            P.emit("vector", lambda e, s=src:
                   e.tensor_add(kpe[0:64, :], s[0:64, :], sraw[0:64, :]),
                   waits=[vw, ("gz", GZ_PADS)])
            P.emit("vector", lambda e, s=src:
                   e.tensor_add(kpo[64:128, :], s[64:128, :],
                                sraw[64:128, :]), inc="rope")

    for blk in range(NB):
        xc_waits = [("ldx", 16 * NC), ("qkmm", P.n("qkmm"))]
        for c in range(NC):
            P.emit("vector", lambda e, c=c, blk=blk: e.tensor_copy(
                xbf[:, BS * c:BS * (c + 1)],
                rs32[:, T * c + BS * blk:T * c + BS * (blk + 1)]),
                waits=xc_waits if c == 0 else [],
                inc="xc" if c == NC - 1 else None)
        qke0 = P.n("qke")
        for g in range(20):
            b = g % 8
            waits = [("xc", P.n("xc")), ("ldw", LDW_ALL),
                     ("r1add", P.n("r1add"))]
            if P.n("qke") - qke0 >= 8:
                waits.append(("qke", P.n("qke") - 7))
            if g < 16:
                jc = g
                for fc in range(NC):
                    P.emit("tensor", lambda e, b=b, jc=jc, fc=fc:
                           e.matmul(pb[b][:, 0:256],
                                    wq_sb[:, 3072 * fc + 128 * jc:
                                          3072 * fc + 128 * (jc + 1)],
                                    xbf[:, BS * fc:BS * (fc + 1)],
                                    start=(fc == 0), stop=(fc == NC - 1)),
                           waits=waits if fc == 0 else [],
                           inc="qkmm" if fc == NC - 1 else None)
                dst, cc = (qraw, jc) if jc < 8 else (kraw, jc - 8)
                P.emit("scalar", lambda e, b=b, dst=dst, cc=cc:
                       e.copy(dst[:, BS * cc:BS * (cc + 1)], pb[b][:, 0:256]),
                       waits=[("qkmm", P.n("qkmm"))], inc="qke")
            else:
                th, jg = (g - 16) // 2, (g - 16) % 2
                for fc in range(NC):
                    P.emit("tensor", lambda e, b=b, th=th, jg=jg, fc=fc:
                           e.matmul(pb[b][:, 0:512],
                                    xbf[:, BS * fc + 128 * th:
                                        BS * fc + 128 * (th + 1)],
                                    wq_sb[:, 3072 * fc + 2048 + 512 * jg:
                                          3072 * fc + 2048 + 512 * (jg + 1)],
                                    start=(fc == 0), stop=(fc == NC - 1)),
                           waits=waits if fc == 0 else [],
                           inc="qkmm" if fc == NC - 1 else None)
                for hh in range(8):
                    h_abs = 8 * jg + hh
                    vdst = vpadE if h_abs % 2 == 0 else vpadO
                    doff = 1024 * th + (h_abs // 2) * 128 + 64 * (h_abs % 2)
                    P.emit("scalar", lambda e, b=b, vdst=vdst, doff=doff,
                           hh=hh:
                           e.copy(vdst[:, doff:doff + 64],
                                  pb[b][:, 64 * hh:64 * (hh + 1)]),
                           waits=[("qkmm", P.n("qkmm"))] if hh == 0 else [],
                           inc="qke" if hh == 7 else None)
        qke_all = P.n("qke")
        rope(qraw, csq_sb, snq_sb, blk, ("qke", qke0 + 8), is_k=False)
        rope(kraw, csk_sb, snk_sb, blk, ("qke", qke0 + 16), is_k=True)

        B = {s: P.n(s) for s in ("sqk", "smax", "sexp", "rcp", "srcp",
                                 "ptmm", "ptea", "avmm", "oevac",
                                 "rope")}
        ROPE2 = B["rope"]

        def s_mm(h):
            par = h % 2
            c = h // 2
            kp = kpe if h % 2 == 0 else kpo
            sbank = pb[par]
            w0 = [("rope", ROPE2), ("gz", GZ_PADS),
                  ("ldc", LDC_ALL), ("qke", qke0 + 18),
                  ("sexp", B["sexp"] + h - 1)]
            P.emit("tensor", lambda e, sbank=sbank: e.matmul(
                sbank[:, 0:384], c_ibf[:], c_mask[:],
                start=True, stop=False), waits=w0)
            P.emit("tensor", lambda e, sbank=sbank, c=c, kp=kp: e.matmul(
                sbank[:, 0:128], qraw[:, BS * c:BS * c + 128],
                kp[:, BS * c:BS * c + 128], start=False, stop=False))
            P.emit("tensor", lambda e, sbank=sbank, c=c, kp=kp: e.matmul(
                sbank[:, 128:384], qraw[:, BS * c + 128:BS * c + 256],
                kp[:, BS * c:BS * (c + 1)], start=False, stop=True),
                inc="sqk")

        def pt_mm(h):
            par = h % 2
            ptb = pb[2 + par]
            wp = [("srcp", B["srcp"] + h + 1),
                  ("ptea", B["ptea"] + h - 1)]
            if h < 2:
                wp.append(("qke", qke0 + 20))
            P.emit("tensor", lambda e, ptb=ptb, par=par: e.matmul(
                ptb[:, 0:128], p_sb[par][:, 0:128], d_sb[par][:, 0:128],
                start=True, stop=True), waits=wp)
            P.emit("tensor", lambda e, ptb=ptb, par=par: e.matmul(
                ptb[:, 128:256], p_sb[par][:, 128:256],
                d_sb[par][:, 128:256], start=True, stop=True))
            P.emit("tensor", lambda e, ptb=ptb, par=par: e.matmul(
                ptb[:, 256:384], p_sb[par][:, 256:384],
                d_sb[par][:, 128:256], start=True, stop=True), inc="ptmm")

        def av_mm(c):
            ob = pb[4 + c % 2]
            wa = [("ptea", B["ptea"] + 2 * c + 2),
                  ("oevac", B["oevac"] + c - 1)]
            if c == 0:
                wa.append(("qke", qke_all))
            P.emit("tensor", lambda e, ob=ob, c=c: e.matmul(
                ob[:, 0:256], vpadE[:, 128 * c:128 * (c + 1)],
                pt_sb[0][:, 0:256], start=True, stop=False), waits=wa)
            P.emit("tensor", lambda e, ob=ob, c=c: e.matmul(
                ob[:, 0:256], vpadO[:, 128 * c:128 * (c + 1)],
                pt_sb[1][:, 0:256], start=False, stop=False))
            P.emit("tensor", lambda e, ob=ob, c=c: e.matmul(
                ob[:, 128:256], vpadE[:, 1024 + 128 * c:1024 + 128 * (c + 1)],
                pt_sb[0][:, 256:384], start=False, stop=False))
            P.emit("tensor", lambda e, ob=ob, c=c: e.matmul(
                ob[:, 128:256], vpadO[:, 1024 + 128 * c:1024 + 128 * (c + 1)],
                pt_sb[1][:, 256:384], start=False, stop=True), inc="avmm")

        def dve_max(h):
            par = h % 2
            sbank = pb[par]
            P.emit("vector", lambda e, sbank=sbank, par=par: e.reduce_max(
                nm_sb[:, 2 * par:2 * par + 1], sbank[:, 0:128], axis=AX.X,
                negate=True),
                waits=[("sqk", B["sqk"] + h + 1),
                       ("sexp", B["sexp"] + h - 1)])
            P.emit("vector", lambda e, sbank=sbank, par=par: e.reduce_max(
                nm_sb[:, 2 * par + 1:2 * par + 2], sbank[:, 128:384],
                axis=AX.X, negate=True), inc="smax")

        def dve_recip(h):
            par = h % 2
            P.emit("vector", lambda e, par=par: e.reciprocal(
                rc_sb[:, 2 * par:2 * par + 2], ss_sb[:, 2 * par:2 * par + 2]),
                waits=[("sexp", B["sexp"] + h + 1)], inc="rcp")

        def pool_d(h, eng="gpsimd"):
            par = h % 2
            P.emit(eng, lambda e, par=par: e.tensor_scalar(
                d_sb[par][:, 0:128], c_ibf[:], rc_sb[:, 2 * par:2 * par + 1],
                None, ALU.mult),
                waits=[("rcp", B["rcp"] + h + 1),
                       ("ptmm", B["ptmm"] + h - 1)])
            P.emit(eng, lambda e, par=par: e.tensor_scalar(
                d_sb[par][:, 128:256], c_ibf[:],
                rc_sb[:, 2 * par + 1:2 * par + 2], None, ALU.mult),
                inc="srcp")

        def act_exp(h):
            par = h % 2
            sbank = pb[par]
            we = [("smax", B["smax"] + h + 1),
                  ("ptmm", B["ptmm"] + h - 1),
                  ("rcp", B["rcp"] + h - 1)]
            P.emit("scalar", lambda e, sbank=sbank, par=par: e.activation(
                p_sb[par][:, 0:128], sbank[:, 0:128], AF.Exp,
                bias=nm_sb[:, 2 * par:2 * par + 1],
                accum_out=ss_sb[:, 2 * par:2 * par + 1]), waits=we)
            P.emit("scalar", lambda e, sbank=sbank, par=par: e.activation(
                p_sb[par][:, 128:384], sbank[:, 128:384], AF.Exp,
                bias=nm_sb[:, 2 * par + 1:2 * par + 2],
                accum_out=ss_sb[:, 2 * par + 1:2 * par + 2]), inc="sexp")

        def pte(h):
            par = h % 2
            ptb = pb[2 + par]
            P.emit("scalar", lambda e, ptb=ptb, par=par: e.copy(
                pt_sb[par][:, 0:384], ptb[:, 0:384]),
                waits=[("ptmm", B["ptmm"] + h + 1),
                       ("avmm", B["avmm"] + h // 2)], inc="ptea")

        def o_evac(c):
            ob = pb[4 + c % 2]
            P.emit("scalar", lambda e, ob=ob, c=c: e.copy(
                oblk[:, BS * c:BS * (c + 1)], ob[:, 0:256]),
                waits=[("avmm", B["avmm"] + c + 1)], inc="oevac")

        if LOOKAHEAD:
            s_mm(0)
            s_mm(1)
            for h in range(NH):
                if h + 2 < NH:
                    s_mm(h + 2)
                pt_mm(h)
                if h % 2 == 1:
                    av_mm(h // 2)
        else:
            for h in range(NH):
                s_mm(h)
                pt_mm(h)
                if h % 2 == 1:
                    av_mm(h // 2)
        dve_max(0)
        dve_max(1)
        for h in range(NH):
            dve_recip(h)
            pool_d(h, eng="vector")
            if h + 2 < NH:
                dve_max(h + 2)
        for h in range(NH):
            act_exp(h)
            if h >= 1:
                pte(h - 1)
                if (h - 1) % 2 == 1:
                    o_evac((h - 1) // 2)
        pte(NH - 1)
        o_evac(NH // 2 - 1)

        for jc in range(NC):
            b = jc % 8
            ww = [("oevac", P.n("oevac")), ("ldw", LDW_ALL)]
            for dc in range(NC):
                P.emit("tensor", lambda e, b=b, jc=jc, dc=dc: e.matmul(
                    pb[b][:, 0:256],
                    wo_sb[:, 1024 * dc + 128 * jc:1024 * dc + 128 * (jc + 1)],
                    oblk[:, BS * dc:BS * (dc + 1)],
                    start=(dc == 0), stop=(dc == NC - 1)),
                    waits=ww if dc == 0 else [],
                    inc="womm" if dc == NC - 1 else None)
            P.emit("vector", lambda e, b=b, jc=jc, blk=blk: e.tensor_add(
                rs32[:, T * jc + BS * blk:T * jc + BS * (blk + 1)],
                rs32[:, T * jc + BS * blk:T * jc + BS * (blk + 1)],
                pb[b][:, 0:256]),
                waits=[("womm", P.n("womm"))], inc="r1add")

    if stage == "attn":
        P.dma("sync", dbg["rs"][:], rs32[:],
              waits=[("r1add", P.n("r1add"))], inc="sto")
        P.emit("gpsimd", lambda e: e.engine_nop(),
               waits=[("sto", P.n("sto"))])
        return nc, P, dbg

    def layer_norm(gate, emit_x1bf, emit_spill):
        cntA = {}

        def stage_a(tg):
            b0, b1 = (6, 7) if tg % 2 == 0 else (4, 5)
            gcols = slice(TT * tg, TT * (tg + 1))
            for c in range(NC):
                sl = c % 2
                wv = [gate(tg) if callable(gate) else gate]
                if P.n("lnpe") >= 2:
                    wv.append(("lnpe", P.n("lnpe") - 1))
                P.emit("scalar", lambda e, c=c, sl=sl, g=gcols: e.copy(
                    lnbf[sl][:], rs32[:, T * c:T * (c + 1)][:, g]),
                    waits=wv, inc="lncast")
                P.emit("scalar", lambda e, sl=sl: e.activation(
                    sqbf[sl][:], lnbf[sl][:], AF.Square),
                    waits=[("lncast", P.n("lncast"))], inc="lnsq")
                mmw = [("lncast", P.n("lncast")), ("ldc", LDC_ALL)]
                if c == 0:
                    mmw += [("lnbce", P.n("lnbce")), ("lnsev", P.n("lnsev"))]
                P.emit("tensor", lambda e, sl=sl, c=c, b0=b0: e.matmul(
                    pb[b0][0:1, 0:TT], c_ones_col[:], lnbf[sl][:],
                    start=(c == 0), stop=(c == NC - 1)), waits=mmw)
                P.emit("tensor", lambda e, sl=sl, c=c, b1=b1: e.matmul(
                    pb[b1][0:1, 0:TT], c_ones_col[:], sqbf[sl][:],
                    start=(c == 0), stop=(c == NC - 1)),
                    waits=[("lnsq", P.n("lnsq"))], inc="lnpe")
            cntA[tg] = P.n("lnpe")

        def stage_b(tg):
            b0, b1 = (6, 7) if tg % 2 == 0 else (4, 5)
            sx = tg % 2
            gcols = slice(TT * tg, TT * (tg + 1))
            sA = srow[0:1, 0:512]
            sB = srow[0:1, 512:1024]
            sM = srow[0:1, 1024:1536]
            sV = srow[0:1, 1536:2048]
            P.emit("vector", lambda e, b0=b0: e.tensor_copy(
                sA, pb[b0][0:1, 0:TT]),
                waits=[("lnpe", cntA[tg]), ("lnst", P.n("lnst"))],
                inc="vself")
            P.emit("vector", lambda e, b1=b1: e.tensor_copy(
                sB, pb[b1][0:1, 0:TT]), inc="lnsev")
            P.emit("vector", lambda e: e.tensor_scalar_mul(sM, sA, 1.0 / D),
                   waits=[("vself", P.n("vself")), ("lnsev", P.n("lnsev")),
                          ("lnbcm", P.n("lnbcm"))], inc="vself")
            P.emit("scalar", lambda e: e.activation(sV, sM, AF.Square),
                   waits=[("vself", P.n("vself")),
                          ("lnbcm", P.n("lnbcm"))], inc="aself")
            P.emit("vector", lambda e: e.scalar_tensor_tensor(
                sV, sB, 1.0 / D, sV, ALU.mult, ALU.subtract),
                waits=[("aself", P.n("aself")), ("lnsev", P.n("lnsev"))],
                inc="lnst")
            P.emit("scalar", lambda e: e.activation(
                sV, sV, AF.Ln, bias=c_eps[0:1, 0:1]),
                waits=[("lnst", P.n("lnst"))], inc="aself")
            P.emit("scalar", lambda e: e.activation(sV, sV, AF.Exp,
                                                    scale=-0.5),
                   waits=[("aself", P.n("aself"))], inc="lnrstd")
            P.emit("tensor", lambda e, b0=b0: e.matmul(
                pb[b0][:, 0:TT], c_ones_row_f[0:1, :], sM, start=True,
                stop=True), waits=[("lnrstd", P.n("lnrstd"))])
            P.emit("tensor", lambda e, b1=b1: e.matmul(
                pb[b1][:, 0:TT], c_ones_row_f[0:1, :], sV, start=True,
                stop=True), inc="lnbcm")
            evw = [("lnbcm", P.n("lnbcm"))]
            if P.n("lnng") >= 8:
                evw.append(("lnng", P.n("lnng") - 4))
            P.emit("vector", lambda e, b0=b0, sx=sx: e.tensor_copy(
                mbc_s[sx][:], pb[b0][:, 0:TT]), waits=evw, inc="vself")
            P.emit("vector", lambda e, b1=b1, sx=sx: e.tensor_copy(
                rbc_s[sx][:], pb[b1][:, 0:TT]), inc="lnbce")
            for c in range(NC):
                view = rs32[:, T * c:T * (c + 1)][:, gcols]
                if c < 4:
                    eng, selfsem, normsem = "vector", "vself", "lnnorm"
                else:
                    eng, selfsem, normsem = "gpsimd", "gself", "lnng"
                P.emit(eng, lambda e, v=view, sx=sx: e.tensor_sub(
                    v, v, mbc_s[sx][:]),
                    waits=[("lnbce", P.n("lnbce")),
                           (selfsem, P.n(selfsem))], inc=selfsem)
                P.emit(eng, lambda e, v=view, sx=sx: e.tensor_mul(
                    v, v, rbc_s[sx][:]),
                    waits=[(selfsem, P.n(selfsem))], inc=normsem)
                if emit_x1bf:
                    P.emit(eng, lambda e, v=view, c=c, g=gcols:
                           e.tensor_copy(x1bf[:, T * c:T * (c + 1)][:, g], v),
                           waits=[(normsem, P.n(normsem))],
                           inc="x1c" if eng == "vector" else "x1cg")

        for tg in range(NTT):
            stage_a(tg)
            if tg >= 1:
                stage_b(tg - 1)
        stage_b(NTT - 1)
        if emit_spill:
            for c in range(NC):
                P.dma("sync", x1d[c, :, :], rs32[:, T * c:T * (c + 1)],
                      waits=[("lnnorm", P.n("lnnorm")),
                             ("lnng", P.n("lnng"))], inc="spx1")

    layer_norm(("r1add", P.n("r1add")), emit_x1bf=True, emit_spill=True)
    LNNORM1 = P.n("lnnorm")
    LNNG1 = P.n("lnng")
    X1C_ALL = P.n("x1c")
    X1CG_ALL = P.n("x1cg")

    for tg in range(NTT):
        gcols = slice(TT * tg, TT * (tg + 1))
        rw = [("lnnorm", LNNORM1), ("lnng", LNNG1), ("ldc", LDC_ALL),
              ("rtev", P.n("rtev"))]
        for k in range(8):
            P.emit("tensor", lambda e, k=k, g=gcols: e.matmul(
                pb[4][0:8, 0:TT], wr_sb[:, 8 * k:8 * (k + 1)],
                rs32[:, T * k:T * (k + 1)][:, g],
                start=(k == 0), stop=(k == 7)),
                waits=rw if k == 0 else [],
                inc="rtmm" if k == 7 else None)
        P.emit("scalar", lambda e, g=gcols: e.activation(
            logits[0:8, g], pb[4][0:8, 0:TT], AF.Identity, bias=br_sb[0:8, 0:1]),
            waits=[("rtmm", P.n("rtmm"))], inc="rtev")
    RTEV_ALL = P.n("rtev")
    for cc in range(16):
        P.emit("tensor", lambda e, cc=cc: e.transpose(
            pb[5][:, 0:8], logits[0:8, 128 * cc:128 * (cc + 1)],
            c_if32[0:8, 0:8]),
            waits=[("rtev", RTEV_ALL), ("rtc", P.n("rtc"))], inc="rtc")
        P.emit("vector", lambda e: e.tensor_copy(lt_tm[:], pb[5][:, 0:8]),
               waits=[("rtc", P.n("rtc"))], inc="rtc")
        P.emit("vector", lambda e, cc=cc: e.reduce_max(
            rnm[:, cc:cc + 1], lt_tm[:], axis=AX.X, negate=True),
            waits=[("rtc", P.n("rtc"))], inc="rtc")
        P.emit("scalar", lambda e, cc=cc: e.activation(
            rexp[:], lt_tm[:], AF.Exp, bias=rnm[:, cc:cc + 1],
            accum_out=res_[:, cc:cc + 1]),
            waits=[("rtc", P.n("rtc"))], inc="rtc")
        P.emit("vector", lambda e, cc=cc: e.reciprocal(
            rgt[:, cc:cc + 1], res_[:, cc:cc + 1]),
            waits=[("rtc", P.n("rtc"))], inc="rtc")
        P.emit("vector", lambda e, cc=cc: e.tensor_scalar_mul(
            rmx[:, cc:cc + 1], rnm[:, cc:cc + 1], -1.0),
            waits=[("rtc", P.n("rtc"))], inc="rtc")
        P.emit("vector", lambda e, cc=cc: e.tensor_scalar(
            lt_m[:], lt_tm[:], rmx[:, cc:cc + 1], None, ALU.is_equal),
            waits=[("rtc", P.n("rtc"))], inc="rtc")
        P.emit("tensor", lambda e, cc=cc: e.transpose(
            pb[5][0:8, 8:136], lt_m[:], c_if32[:]),
            waits=[("rtc", P.n("rtc"))], inc="rtc")
        P.emit("vector", lambda e, cc=cc: e.tensor_copy(
            logits[0:8, 128 * cc:128 * (cc + 1)], pb[5][0:8, 8:136]),
            waits=[("rtc", P.n("rtc"))], inc="rtc")
        P.emit("tensor", lambda e, cc=cc: e.transpose(
            pb[5][0:1, 136:264], rgt[:, cc:cc + 1], c_if32[:]),
            waits=[("rtc", P.n("rtc"))], inc="rtc")
        P.emit("vector", lambda e, cc=cc: e.tensor_copy(
            grow[0:1, 128 * cc:128 * (cc + 1)], pb[5][0:1, 136:264]),
            waits=[("rtc", P.n("rtc"))], inc="rtc")
    RTC_ALL = P.n("rtc")
    for tg in range(NTT):
        P.emit("tensor", lambda e, tg=tg: e.matmul(
            pb[5][:, 0:TT], c_ones_row[0:1, :],
            grow[0:1, TT * tg:TT * (tg + 1)], start=True, stop=True),
            waits=[("rtc", RTC_ALL), ("gbce", P.n("gbce"))], inc="gbcm")
        P.emit("vector", lambda e, tg=tg: e.tensor_copy(
            gatebc[:, TT * tg:TT * (tg + 1)], pb[5][:, 0:TT]),
            waits=[("gbcm", P.n("gbcm"))], inc="gbce")
    GBCE_ALL = P.n("gbce")

    if stage == "ln1":
        P.dma("sync", dbg["rs"][:], rs32[:],
              waits=[("lnnorm", LNNORM1)], inc="sto")
        P.dma("gpsimd", dbg["aux"][:, 0:T], gatebc[:],
              waits=[("gbce", GBCE_ALL)], inc="sto")
        P.dma("sync", dbg["aux"][0:8, T:2 * T], logits[0:8, :],
              waits=[("rtc", RTC_ALL)], inc="sto")
        P.dma("sync", dbg["aux"][:, 2 * T:2 * T + TT], mbc[:],
              waits=[("lnnorm", LNNORM1)], inc="sto")
        P.dma("sync", dbg["aux"][:, 2 * T + TT:2 * T + 2 * TT], rbc[:],
              waits=[("lnnorm", LNNORM1)], inc="sto")
        P.dma("sync", dbg["aux"][0:1, 2 * T + 2 * TT:2 * T + 2 * TT + 2048],
              srow[0:1, :], waits=[("lnnorm", LNNORM1)], inc="sto")
        P.emit("gpsimd", lambda e: e.engine_nop(),
               waits=[("sto", P.n("sto"))])
        return nc, P, dbg

    P.emit("gpsimd", lambda e: e.memset(ybf[:], 0.0),
           waits=[("r1add", P.n("r1add"))], inc="ybz")

    for e_ in range(E):
        par = e_ % 2
        P.dma("sync", b1_sb[:, 32 * par:32 * (par + 1)],
              d_in["b1"][e_, :, :],
              waits=[("h1ev", max(0, P.n("h1ev") - 128))], inc=f"b1ld{par}")
        P.dma("sync", b2_sb[:, 8 * par:8 * (par + 1)],
              d_in["b2"][e_, :, :],
              waits=[("yeev", max(0, P.n("yeev") - 32))], inc=f"b2ld{par}")
        P.dma("gpsimd", mrow[0:1, :], logits[e_:e_ + 1, :],
              waits=[("rtc", RTC_ALL), ("mbcm", P.n("mbcm"))], inc="mrld")
        for tg in range(NTT):
            bnk = 6 + tg % 2
            P.emit("tensor", lambda e, tg=tg, bnk=bnk: e.matmul(
                pb[bnk][:, 0:TT], c_ones_row[0:1, :],
                mrow[0:1, TT * tg:TT * (tg + 1)], start=True, stop=True),
                waits=[("mrld", P.n("mrld")), ("lnbce", P.n("lnbce")),
                       ("mbce", max(0, P.n("mbce") - 1)),
                       ("h1ev", P.n("h1ev")), ("yeev", P.n("yeev"))],
                inc="mbcm")
            P.emit("vector", lambda e, tg=tg, par=par, bnk=bnk: e.tensor_copy(
                masku8[par][:, TT * tg:TT * (tg + 1)], pb[bnk][:, 0:TT]),
                waits=[("mbcm", P.n("mbcm")),
                       ("comb", max(0, P.n("comb") - 32))], inc="mbce")
        spill_counts = {}
        hld_wait = {}
        for fo in range(NFO):
            sl = fo % 2
            P.dma("gpsimd", w1slab[sl][:], d_in["w1"][e_, fo, :, :],
                  waits=[("ph1mm", max(0, P.n("ph1mm") - 1)),
                         ("lncast", P.n("lncast")),
                         ("lnpe", P.n("lnpe")), ("lnsq", P.n("lnsq"))],
                  inc=f"w1ld{sl}")
            if fo < 16:
                if e_ == 0:
                    w2w = ([("rtmm", P.n("rtmm")), ("spx1", P.n("spx1")),
                            ("lnnorm", LNNORM1), ("lnng", LNNG1),
                            ("x1c", X1C_ALL), ("x1cg", X1CG_ALL)]
                           if fo == 0 else [])
                else:
                    w2w = [("ph2", 128 * e_)] if fo == 0 else []
                P.dma("gpsimd", w2buf[:, 2048 * fo:2048 * (fo + 1)],
                      d_in["w2"][e_, :, 2048 * fo:2048 * (fo + 1)],
                      waits=w2w, inc="w2ld")
            mw = [(f"w1ld{sl}", P.n(f"w1ld{sl}")),
                  ("h1ev", max(0, P.n("h1ev") - 4))]
            if not (e_ == 0 and fo == 0):
                mw += [("x1c", X1C_ALL), ("x1cg", X1CG_ALL)]
            if fo < 2:
                mw += [("yeev", P.n("yeev")), ("mbce", P.n("mbce"))]
            if fo < 2 and e_ == 0:
                mw += [("lnbce", P.n("lnbce")), ("rtc", RTC_ALL),
                       ("gbce", GBCE_ALL)]
            for k in range(8):
                for g in range(4):
                    gw = list(mw) if (k == 0 and g == 0) else []
                    if fo == 0 and k == 0 and e_ == 0:
                        gw.append(("x1c", min(4 * (g + 1), X1C_ALL)))
                        gw.append(("x1cg", min(4 * (g + 1), X1CG_ALL)))
                    P.emit("tensor", lambda e, sl=sl, k=k, g=g, fo=fo:
                           e.matmul(pb[4 * (fo % 2) + g][:, 0:TT],
                                    w1slab[sl][:, 128 * k:128 * (k + 1)],
                                    x1bf[:, T * k + TT * g:T * k + TT * (g + 1)],
                                    start=(k == 0), stop=(k == 7)),
                           waits=gw,
                           inc="ph1mm" if (k == 7 and g == 3) else None)
            for g in range(4):
                P.emit("scalar", lambda e, sl=sl, g=g, fo=fo, par=par:
                       e.activation(h1stage[sl][:, TT * g:TT * (g + 1)],
                                    pb[4 * (fo % 2) + g][:, 0:TT], AF.Relu,
                                    bias=b1_sb[:, 32 * par + fo:
                                               32 * par + fo + 1]),
                       waits=[("ph1mm", P.n("ph1mm")),
                              (f"b1ld{par}", P.n(f"b1ld{par}")),
                              (f"h1sp{sl}", P.n(f"h1sp{sl}"))]
                       if g == 0 else [],
                       inc="h1ev")
            P.emit("sync", lambda e, sl=sl, fo=fo: e.dma_start(
                out=h1d[:, fo, :, :].rearrange("g p t -> p g t"),
                in_=h1stage[sl][:].rearrange("p (g t) -> p g t", g=4)),
                waits=[("h1ev", P.n("h1ev"))], inc=f"h1sp{sl}", inck=16)
            spill_counts[fo] = P.n(f"h1sp{sl}")

        for fo in range(8):
            P.dma("gpsimd", h1in[fo][:], h1d[0, fo, :, :],
                  waits=[(f"h1sp{fo % 2}", spill_counts[fo]),
                         ("ph2", max(0, P.n("ph2") - 7 + fo))],
                  inc=f"h1lp{fo}")
            hld_wait[fo] = (f"h1lp{fo}", P.n(f"h1lp{fo}"))
        for tt in range(NTT):
            for fo in range(NFO):
                idx = tt * NFO + fo
                buf = idx % 8
                if idx >= 8:
                    P.dma("sync", h1in[buf][:], h1d[tt, fo, :, :],
                          waits=[(f"h1sp{fo % 2}", spill_counts[fo]),
                                 ("ph2", max(0, P.n("ph2") - 7))],
                          inc=f"h1ld{buf}")
                    hld_wait[idx] = (f"h1ld{buf}", P.n(f"h1ld{buf}"))
                for h in range(8):
                    mw2 = [hld_wait[idx]]
                    if fo == 0:
                        mw2.append(("w2ld", P.n("w2ld")))
                        if tt == 0:
                            mw2 += [("h1ev", P.n("h1ev")),
                                    ("mbce", P.n("mbce"))]
                        else:
                            mw2.append(("yeev", P.n("yeev") - 7 + h))
                    P.emit("tensor", lambda e, h=h, fo=fo, buf=buf: e.matmul(
                        pb[h][:, 0:TT],
                        w2buf[:, 128 * (8 * fo + h):128 * (8 * fo + h + 1)],
                        h1in[buf][:], start=(fo == 0), stop=(fo == NFO - 1)),
                        waits=mw2,
                        inc="ph2" if h == 7 else None)
            for h in range(8):
                sbuf_ = yestage[h % 4]
                P.emit("scalar", lambda e, h=h, par=par, sbuf_=sbuf_:
                       e.activation(sbuf_[:], pb[h][:, 0:TT], AF.Identity,
                                    bias=b2_sb[:, 8 * par + h:
                                               8 * par + h + 1]),
                       waits=[("ph2", P.n("ph2")),
                              (f"b2ld{par}", P.n(f"b2ld{par}")),
                              ("comb", max(0, P.n("comb") - 3))],
                       inc="yeev")
                P.emit("vector", lambda e, h=h, tt=tt, par=par, sbuf_=sbuf_:
                       e.copy_predicated(
                           ybf[:, T * h + TT * tt:T * h + TT * (tt + 1)],
                           masku8[par][:, TT * tt:TT * (tt + 1)], sbuf_[:]),
                       waits=[("yeev", P.n("yeev")), ("ybz", 1),
                              ("mbce", P.n("mbce"))], inc="comb")

    COMB_ALL = P.n("comb")
    for c in range(NC):
        P.dma("sync", rs32[:, T * c:T * (c + 1)], x1d[c, :, :],
              waits=[("ph2", P.n("ph2")), ("comb", COMB_ALL)]
              if c == 0 else [], inc="x1ld")
    X1LD_ALL = P.n("x1ld")
    for c in range(NC):
        P.emit("vector", lambda e, c=c: e.tensor_tensor(
            ybf[:, T * c:T * (c + 1)], ybf[:, T * c:T * (c + 1)],
            gatebc[:], ALU.mult),
            waits=[("comb", COMB_ALL), ("gbce", GBCE_ALL)]
            if c == 0 else [], inc="tailmul")
    TMUL = P.n("tailmul")
    for tg in range(NTT):
        for c in range(NC):
            sl_ = slice(T * c + TT * tg, T * c + TT * (tg + 1))
            P.emit("vector", lambda e, s=sl_: e.tensor_add(
                rs32[:, s], rs32[:, s], ybf[:, s]),
                waits=[("x1ld", X1LD_ALL), ("tailmul", TMUL)]
                if (tg == 0 and c == 0) else [], inc="tailadd")
    TA0 = P.n("tailadd") - 4 * NC
    layer_norm(lambda tg: ("tailadd", TA0 + NC * (tg + 1)),
               emit_x1bf=False, emit_spill=False)
    for c in range(NC):
        P.dma("sync", y_d[128 * c:128 * (c + 1), :],
              rs32[:, T * c:T * (c + 1)],
              waits=[("lnnorm", P.n("lnnorm")), ("lnng", P.n("lnng"))]
              if c == 0 else [], inc="sto")
    P.emit("gpsimd", lambda e: e.engine_nop(),
           waits=[("sto", P.n("sto"))])
    return nc, P, dbg


_CACHED = {}


def _get_nc(stage="full"):
    if stage not in _CACHED:
        nc, P, dbg = build_program(stage)
        P.realize(nc)
        _CACHED[stage] = nc
    return _CACHED[stage]


def kernel(**inputs):
    maps = prep_inputs(**inputs)
    nc = _get_nc("full")
    res = run_bass_kernel_spmd(nc, maps, core_ids=list(range(8)))
    outs = []
    for r in res.results:
        y = np.asarray(r["y"])
        outs.append(y.T)
    return np.ascontiguousarray(np.stack(outs, axis=0))
